# version 02
# baseline (speedup 1.0000x reference)
import numpy as np
import ml_dtypes
import concourse.bass as bass
import concourse.mybir as mybir
from concourse.bass_utils import run_bass_kernel_spmd

F32 = mybir.dt.float32
BF16 = mybir.dt.bfloat16
AF = mybir.ActivationFunctionType
ALU = mybir.AluOpType
AX = mybir.AxisListType
NBF = ml_dtypes.bfloat16

NCORES = 8
NSPAN = 4
SPAN = 2048
TT = 512
NTT = SPAN // TT
D = 1024
DCH = D // 128


class _Op:
    __slots__ = ("eng", "fn", "reads", "writes", "sreads", "is_dma", "idx",
                 "need_inc", "sem", "sem_val", "waits", "tag")
    def __init__(self, eng, fn, reads, writes, sreads, is_dma, idx, tag):
        self.eng = eng; self.fn = fn; self.reads = reads; self.writes = writes
        self.sreads = sreads; self.is_dma = is_dma; self.idx = idx
        self.need_inc = is_dma
        self.sem = None
        self.sem_val = None; self.waits = []; self.tag = tag

def _rname(x):
    if isinstance(x, str):
        return x
    return x.name

class Prog:
    ENGS = ("sync", "gpsimd", "vector", "scalar", "tensor")
    CENGS = ("gpsimd", "vector", "scalar", "tensor")

    DEFAULT_STRICT = ("vector", "scalar", "gpsimd")

    def __init__(self, strict=None):
        self.nc = bass.Bass("TRN2", target_bir_lowering=False)
        self.strict = self.DEFAULT_STRICT if strict is None else strict
        self.ops = []
        self.last_writer = {}
        self.readers = {}
        self.slot_of = {}
        self.slot_merge = {}
        self.psum_names = set()

    def _st(self, eng):
        if self.strict is True:
            return True
        if not self.strict:
            return False
        return eng in self.strict

    def dram(self, name, shape, dtype, kind="Internal", **kw):
        return self.nc.dram_tensor(name, list(shape), dtype, kind=kind, **kw)

    def sb(self, name, shape, dtype):
        return self.nc.alloc_sbuf_tensor(name, list(shape), dtype)

    def psum(self, name, shape, dtype=F32):
        self.psum_names.add(name)
        return self.nc.alloc_psum_tensor(name, list(shape), dtype)

    def op(self, eng, fn, r=(), w=(), sr=(), dma=False, tag=""):
        o = _Op(eng, fn, [_rname(x) for x in r], [_rname(x) for x in w],
                [_rname(x) for x in sr], dma, len(self.ops), tag)
        if dma:
            slot = self.slot_merge.get(o.writes[0], o.writes[0])
            o.sem = "D_" + slot
            self.slot_of[o.sem] = self.slot_of.get(o.sem, 0) + 16
            o.sem_val = self.slot_of[o.sem]
        else:
            o.sem = o.eng
        deps = []
        for res in o.reads + o.sreads:
            lw = self.last_writer.get(res)
            if lw is not None:
                scalar = res in o.sreads
                if lw.eng != eng or scalar or lw.is_dma or self._st(eng):
                    deps.append(lw)
            if res in self.psum_names:
                for rd in self.readers.get(res, ()):
                    if rd.eng != eng:
                        deps.append(rd)
        for res in o.writes:
            lw = self.last_writer.get(res)
            if lw is not None and (lw.eng != eng or lw.is_dma or dma or self._st(eng)):
                deps.append(lw)
            for rd in self.readers.get(res, ()):
                if rd.eng != eng or dma or self._st(eng):
                    deps.append(rd)
        seen = set()
        for d in deps:
            if d is o or id(d) in seen:
                continue
            seen.add(id(d))
            d.need_inc = True
            o.waits.append(d)
        for res in o.writes:
            self.last_writer[res] = o
            self.readers[res] = []
        for res in o.reads + o.sreads:
            self.readers.setdefault(res, []).append(o)
        self.ops.append(o)
        return o

    def dio(self, queue, out_ap, in_ap, w=(), r=(), tag=""):
        return self.op(queue, lambda e, o=out_ap, i=in_ap: e.dma_start(out=o, in_=i),
                       r=[in_ap] + list(r), w=[out_ap] + list(w), dma=True, tag=tag)

    def finish(self):
        o = _Op("sync", lambda e: e.nop(), [], [], [], False, len(self.ops), "finish")
        for op in self.ops:
            if op.is_dma:
                op.need_inc = True
        last_per_slot = {}
        for op in self.ops:
            if op.is_dma:
                last_per_slot[op.sem] = op
        o.waits = list(last_per_slot.values())
        self.ops.append(o)

    def build(self, verbose=False):
        nc = self.nc
        counts = {e: 0 for e in self.ENGS}
        for o in self.ops:
            if o.need_inc and not o.is_dma:
                counts[o.eng] += 1
                o.sem_val = counts[o.eng]
        sem_names = list(self.ENGS) + sorted(self.slot_of.keys())
        avail = len(range(150, 256)) - 2
        if len(sem_names) > avail:
            raise RuntimeError(f"too many semaphores: {len(sem_names)}")
        import contextlib
        with contextlib.ExitStack() as st:
            sems = {n: st.enter_context(nc.semaphore(n[:31])) for n in sem_names}
            block = st.enter_context(nc.Block())
            nwaits = 0
            for eng in self.ENGS:
                ops = [o for o in self.ops if o.eng == eng]
                if not ops:
                    continue
                plan = []
                hi = {}
                for o in ops:
                    ws = []
                    for d in o.waits:
                        v = d.sem_val
                        assert v is not None, (o.tag, d.tag)
                        if hi.get(d.sem, 0) >= v:
                            continue
                        hi[d.sem] = v
                        ws.append((d.sem, v))
                    plan.append((ws, o))
                    nwaits += len(ws)
                def make(plan, sems=sems):
                    def body(e):
                        for ws, o in plan:
                            for sname, v in ws:
                                e.wait_ge(sems[sname], v)
                            instr = o.fn(e)
                            if o.need_inc:
                                instr.then_inc(sems[o.sem], 16 if o.is_dma else 1)
                    return body
                getattr(block, eng)(make(plan))
            if verbose:
                n_inc = sum(1 for o in self.ops if o.need_inc)
                print(f"[Prog] ops={len(self.ops)} incs={n_inc} waits={nwaits} sems={len(sem_names)}")
        return nc

def cm(c):
    return c // NSPAN, c % NSPAN

def fmj(x):
    return np.ascontiguousarray(np.asarray(x).T)

def pcol(w):
    return np.ascontiguousarray(np.asarray(w).reshape(-1, 128).T)


def e_ln(p, pre, zbf, ntok, rows, scr, eps=1e-5):
    sq, ps_s, ps_q, onec = scr["sq"], scr["ps_s"], scr["ps_q"], scr["onec"]
    srow, qrow, rstd, negmur = rows["srow"], rows["qrow"], rows["rstd"], rows["negmur"]
    nch = len(zbf)
    for i, zb in enumerate(zbf):
        s = sq[i % 2]
        p.op("scalar", lambda e, o=s, x=zb, n=ntok: e.activation(o[:, 0:n], x[:, 0:n], AF.Square),
             r=[zb], w=[s], tag=f"{pre}sq{i}")
        p.op("tensor", lambda e, x=zb, n=ntok, f=(i == 0), l=(i == nch - 1):
             e.matmul(ps_s[0:1, 0:n], onec[:, 0:1], x[:, 0:n], start=f, stop=l),
             r=[zb, onec], w=[ps_s], tag=f"{pre}mms{i}")
        p.op("tensor", lambda e, x=s, n=ntok, f=(i == 0), l=(i == nch - 1):
             e.matmul(ps_q[0:1, 0:n], onec[:, 0:1], x[:, 0:n], start=f, stop=l),
             r=[s, onec], w=[ps_q], tag=f"{pre}mmq{i}")
    p.op("vector", lambda e, n=ntok: e.tensor_scalar_mul(srow[0:1, 0:n], ps_s[0:1, 0:n], 1.0 / D),
         r=[ps_s], w=[srow])
    p.op("vector", lambda e, n=ntok: e.tensor_scalar_mul(qrow[0:1, 0:n], ps_q[0:1, 0:n], 1.0 / D),
         r=[ps_q], w=[qrow])
    p.op("vector", lambda e, n=ntok: e.tensor_mul(rstd[0:1, 0:n], srow[0:1, 0:n], srow[0:1, 0:n]),
         r=[srow], w=[rstd])
    p.op("vector", lambda e, n=ntok: e.tensor_sub(qrow[0:1, 0:n], qrow[0:1, 0:n], rstd[0:1, 0:n]),
         r=[qrow, rstd], w=[qrow])
    p.op("vector", lambda e, n=ntok: e.tensor_scalar_add(qrow[0:1, 0:n], qrow[0:1, 0:n], float(eps)),
         r=[qrow], w=[qrow])
    p.op("scalar", lambda e, n=ntok: e.activation(rstd[0:1, 0:n], qrow[0:1, 0:n], AF.Ln),
         r=[qrow], w=[rstd])
    p.op("scalar", lambda e, n=ntok: e.activation(rstd[0:1, 0:n], rstd[0:1, 0:n], AF.Exp, scale=-0.5),
         r=[rstd], w=[rstd])
    p.op("vector", lambda e, n=ntok: e.tensor_mul(negmur[0:1, 0:n], srow[0:1, 0:n], rstd[0:1, 0:n]),
         r=[srow, rstd], w=[negmur])
    p.op("vector", lambda e, n=ntok: e.tensor_scalar_mul(negmur[0:1, 0:n], negmur[0:1, 0:n], -1.0),
         r=[negmur], w=[negmur])

def e_bc2(p, rows, scr, ntok):
    oner, ps_r, ps_m = scr["oner"], scr["ps_r"], scr["ps_m"]
    rstd, negmur = rows["rstd"], rows["negmur"]
    p.op("tensor", lambda e, n=ntok: e.matmul(ps_r[:, 0:n], oner[0:1, :], rstd[0:1, 0:n],
                                              start=True, stop=True), r=[rstd, oner], w=[ps_r])
    p.op("tensor", lambda e, n=ntok: e.matmul(ps_m[:, 0:n], oner[0:1, :], negmur[0:1, 0:n],
                                              start=True, stop=True), r=[negmur, oner], w=[ps_m])


_BUILD_CACHE = {}

def _cached(key, fn):
    if key not in _BUILD_CACHE:
        _BUILD_CACHE[key] = fn().build()
    return _BUILD_CACHE[key]


L0H = 3

def build_l0_pre(strict=None, cut=99, ntiles=99):
    p = Prog(strict)
    HT = L0H + SPAN
    INNER, ICH = 2048, 16

    res = p.dram("res", [D, HT], F32, kind="ExternalInput")
    hval = p.dram("hval", [128, 1], F32, kind="ExternalInput")
    w_in = p.dram("w_in", [128, DCH * 4096], F32, kind="ExternalInput")
    b_in = p.dram("b_in", [128, 32], F32, kind="ExternalInput")
    w_lnw = p.dram("w_lnw", [128, DCH], F32, kind="ExternalInput")
    w_lnb = p.dram("w_lnb", [128, DCH], F32, kind="ExternalInput")
    w_cw = p.dram("w_cw", [128, ICH * 4], F32, kind="ExternalInput")
    w_cb = p.dram("w_cb", [128, ICH], F32, kind="ExternalInput")
    w_xp = p.dram("w_xp", [128, ICH * 96], F32, kind="ExternalInput")
    w_dtp = p.dram("w_dtp", [64, INNER], F32, kind="ExternalInput")
    b_dtp = p.dram("b_dtp", [128, 16], F32, kind="ExternalInput")

    z_out = p.dram("z_out", [D, SPAN], F32, kind="ExternalOutput")
    xs_out = p.dram("xs_out", [INNER, SPAN], BF16, kind="ExternalOutput")
    dt_out = p.dram("dt_out", [INNER, SPAN], F32, kind="ExternalOutput")
    bc_out = p.dram("bc_out", [32, SPAN], F32, kind="ExternalOutput")
    gate_out = p.dram("gate_out", [INNER, SPAN], BF16, kind="ExternalOutput")

    w_in_bf = p.dram("w_in_bf", [128, DCH * 4096], BF16)

    rest = [p.sb(f"rest{i}", [128, TT], F32) for i in range(DCH)]
    zt = [p.sb(f"zt{i}", [128, TT], F32) for i in range(DCH)]
    z2t = [p.sb(f"z2t{i}", [128, TT], F32) for i in range(DCH)]
    zbf = [p.sb(f"zbf{i}", [128, TT], BF16) for i in range(DCH)]
    hnb = [p.sb(f"hnb{i}", [128, TT], BF16) for i in range(DCH)]
    xcur = [p.sb(f"xcur{i}", [128, L0H + TT], BF16) for i in range(ICH)]
    xhalo = [p.sb(f"xhalo{i}", [128, L0H], BF16) for i in range(ICH)]
    xsb = [p.sb(f"xsb{i}", [128, TT], BF16) for i in range(ICH)]
    convt = p.sb("convt", [128, TT], F32)
    dtraw = p.sb("dtraw", [64, TT], BF16)
    bcrow = p.sb("bcrow", [32, TT], F32)
    sq2 = [p.sb(f"sq{i}", [128, TT], BF16) for i in range(2)]
    rowset = [{k: p.sb(f"r{j}_{k}", [1, TT], F32) for k in ("srow", "qrow", "rstd", "negmur")}
              for j in range(2)]
    onec = p.sb("onec", [128, 1], BF16)
    oner = p.sb("oner", [1, 128], F32)
    lnw = p.sb("lnw", [128, DCH], F32)
    lnb = p.sb("lnb", [128, DCH], F32)
    inb = p.sb("inb", [128, 32], F32)
    cwt = p.sb("cwt", [128, ICH * 4], F32)
    cbt = p.sb("cbt", [128, ICH], F32)
    xpt = p.sb("xpt", [128, ICH * 96], BF16)
    dtpt = p.sb("dtpt", [64, INNER], BF16)
    dtpb = p.sb("dtpb", [128, 16], F32)
    hvt = p.sb("hvt", [128, 1], F32)
    wblk = [p.sb(f"wblk{i}", [128, 1024], BF16) for i in range(2)]
    spl = p.sb("spl", [128, TT], F32)
    spr = p.sb("spr", [128, TT], F32)

    ps_s = p.psum("ps_s", [1, TT], F32)
    ps_q = p.psum("ps_q", [1, TT], F32)
    ps_r = p.psum("ps_r", [128, TT], F32)
    ps_m = p.psum("ps_m", [128, TT], F32)
    ps_mm = [p.psum(f"ps_mm{i}", [128, TT], F32) for i in range(2)]
    ps_xp = p.psum("ps_xp", [96, TT], F32)

    scr = {"sq": sq2, "ps_s": ps_s, "ps_q": ps_q, "onec": onec,
           "oner": oner, "ps_r": ps_r, "ps_m": ps_m}

    for t, s in [(lnw, w_lnw), (lnb, w_lnb), (inb, b_in), (cwt, w_cw), (cbt, w_cb),
                 (dtpb, b_dtp), (hvt, hval)]:
        p.dio("sync", t[:], s[:])
    p.dio("gpsimd", xpt[:], w_xp[:])
    p.dio("gpsimd", dtpt[:], w_dtp[:])
    p.dio("gpsimd", w_in_bf[:], w_in[:])
    p.op("vector", lambda e: e.memset(onec[:], 1.0), w=[onec])
    p.op("vector", lambda e: e.memset(oner[:], 1.0), w=[oner])

    tiles = [(0, L0H)] + [(L0H + i * TT, TT) for i in range(NTT)]
    tiles = tiles[:1 + min(NTT, ntiles)]

    for ti, (c0, ntok) in enumerate(tiles):
        is_halo = (ti == 0)
        for i in range(DCH):
            p.dio("sync", rest[i][:, 0:ntok], res[128 * i:128 * (i + 1), c0:c0 + ntok])
        for i in range(DCH):
            p.op("vector", lambda e, i=i, n=ntok: e.tensor_copy(zbf[i][:, 0:n], rest[i][:, 0:n]),
                 r=[rest[i]], w=[zbf[i]])
        e_ln(p, f"t{ti}a", zbf, ntok, rowset[0], scr)
        e_bc2(p, rowset[0], scr, ntok)
        for i in range(DCH):
            p.op("vector", lambda e, i=i, n=ntok: e.tensor_mul(zt[i][:, 0:n], rest[i][:, 0:n], ps_r[:, 0:n]),
                 r=[rest[i], ps_r], w=[zt[i]])
            p.op("vector", lambda e, i=i, n=ntok: e.tensor_add(zt[i][:, 0:n], zt[i][:, 0:n], ps_m[:, 0:n]),
                 r=[zt[i], ps_m], w=[zt[i]])
            p.op("vector", lambda e, i=i, n=ntok: e.tensor_copy(zbf[i][:, 0:n], zt[i][:, 0:n]),
                 r=[zt[i]], w=[zbf[i]])
        if not is_halo:
            for i in range(DCH):
                p.dio("sync", z_out[128 * i:128 * (i + 1), c0 - L0H:c0 - L0H + ntok], zt[i][:, 0:ntok])
        if cut < 2:
            continue
        e_ln(p, f"t{ti}b", zbf, ntok, rowset[1], scr)
        e_bc2(p, rowset[1], scr, ntok)
        for i in range(DCH):
            p.op("vector", lambda e, i=i, n=ntok: e.tensor_mul(z2t[i][:, 0:n], zt[i][:, 0:n], ps_r[:, 0:n]),
                 r=[zt[i], ps_r], w=[z2t[i]])
            p.op("vector", lambda e, i=i, n=ntok: e.tensor_add(z2t[i][:, 0:n], z2t[i][:, 0:n], ps_m[:, 0:n]),
                 r=[z2t[i], ps_m], w=[z2t[i]])
            p.op("vector", lambda e, i=i, n=ntok: e.tensor_scalar(hnb[i][:, 0:n], z2t[i][:, 0:n],
                                                                  lnw[:, i:i + 1], lnb[:, i:i + 1],
                                                                  ALU.mult, ALU.add),
                 r=[z2t[i]], sr=[lnw, lnb], w=[hnb[i]])

        for m in range(32):
            wb = wblk[m % 2]
            p.dio("sync", wb[:], w_in_bf[:, m * 1024:(m + 1) * 1024], tag=f"win{m}")
            bank = ps_mm[m % 2]
            for k in range(DCH):
                p.op("tensor", lambda e, k=k, m=m, wb=wb, bank=bank, n=ntok:
                     e.matmul(bank[:, 0:n], wb[:, k * 128:(k + 1) * 128], hnb[k][:, 0:n],
                              start=(k == 0), stop=(k == DCH - 1)),
                     r=[wb, hnb[k]], w=[bank], tag=f"inproj{m}")
            if m < ICH:
                dst = xhalo[m] if is_halo else xcur[m]
                off = 0 if is_halo else L0H
                p.op("scalar", lambda e, m=m, dst=dst, off=off, bank=bank, n=ntok:
                     e.add(dst[:, off:off + n], bank[:, 0:n], inb[:, m:m + 1]),
                     r=[bank], sr=[inb], w=[dst], tag=f"xevac{m}")
                if is_halo:
                    p.op("vector", lambda e, m=m, n=ntok:
                         e.tensor_scalar_mul(xhalo[m][:, 0:n], xhalo[m][:, 0:n], hvt[0:128, 0:1]),
                         r=[xhalo[m]], sr=[hvt], w=[xhalo[m]])
            else:
                mg = m - ICH
                p.op("scalar", lambda e, mg=mg, bank=bank, n=ntok:
                     e.activation(sq2[0][:, 0:n], bank[:, 0:n], AF.Silu, bias=inb[:, mg + 16:mg + 17]),
                     r=[bank], sr=[inb], w=[sq2[0]], tag=f"gevac{mg}")
                if not is_halo:
                    p.dio("sync", gate_out[128 * mg:128 * (mg + 1), c0 - L0H:c0 - L0H + ntok],
                          sq2[0][:, 0:ntok])

        if is_halo or cut < 3:
            continue

        for i in range(ICH):
            p.op("vector", lambda e, i=i: e.tensor_copy(xcur[i][:, 0:L0H], xhalo[i][:, 0:L0H]),
                 r=[xhalo[i]], w=[xcur[i]])
            for j in range(4):
                if j == 0:
                    p.op("vector", lambda e, i=i, n=ntok:
                         e.tensor_scalar_mul(convt[:, 0:n], xcur[i][:, 0:n], cwt[:, 4 * i:4 * i + 1]),
                         r=[xcur[i]], sr=[cwt], w=[convt], tag=f"conv{i}.0")
                else:
                    p.op("vector", lambda e, i=i, j=j, n=ntok:
                         e.scalar_tensor_tensor(convt[:, 0:n], xcur[i][:, j:j + n], cwt[:, 4 * i + j:4 * i + j + 1],
                                                convt[:, 0:n], ALU.mult, ALU.add),
                         r=[xcur[i], convt], sr=[cwt], w=[convt], tag=f"conv{i}.{j}")
            p.op("vector", lambda e, i=i, n=ntok: e.tensor_copy(xhalo[i][:, 0:L0H], xcur[i][:, n:n + L0H]),
                 r=[xcur[i]], w=[xhalo[i]])
            p.op("scalar", lambda e, i=i, n=ntok: e.activation(xsb[i][:, 0:n], convt[:, 0:n],
                                                               AF.Silu, bias=cbt[:, i:i + 1]),
                 r=[convt], sr=[cbt], w=[xsb[i]], tag=f"silu{i}")
            p.dio("sync", xs_out[128 * i:128 * (i + 1), c0 - L0H:c0 - L0H + ntok], xsb[i][:, 0:ntok])

        if cut < 4:
            continue
        for k in range(ICH):
            p.op("tensor", lambda e, k=k, n=ntok:
                 e.matmul(ps_xp[:, 0:n], xpt[:, 96 * k:96 * (k + 1)], xsb[k][:, 0:n],
                          start=(k == 0), stop=(k == ICH - 1)),
                 r=[xpt, xsb[k]], w=[ps_xp], tag=f"xp{k}")
        p.op("vector", lambda e, n=ntok: e.tensor_copy(dtraw[:, 0:n], ps_xp[0:64, 0:n]),
             r=[ps_xp], w=[dtraw])
        p.op("vector", lambda e, n=ntok: e.tensor_copy(bcrow[:, 0:n], ps_xp[64:96, 0:n]),
             r=[ps_xp], w=[bcrow])
        p.dio("sync", bc_out[:, c0 - L0H:c0 - L0H + ntok], bcrow[:, 0:ntok])

        if cut < 5:
            continue
        for m in range(ICH):
            bank = ps_mm[m % 2]
            p.op("tensor", lambda e, m=m, bank=bank, n=ntok:
                 e.matmul(bank[:, 0:n], dtpt[:, 128 * m:128 * (m + 1)], dtraw[:, 0:n],
                          start=True, stop=True),
                 r=[dtpt, dtraw], w=[bank], tag=f"dtp{m}")
            if cut >= 6:
                import os as _os
                _v = _os.environ.get("L0CUT6", "all")
                if _v in ("all", "abs"):
                    p.op("scalar", lambda e, m=m, bank=bank, n=ntok:
                         e.activation(spl[:, 0:n], bank[:, 0:n], AF.Abs, bias=dtpb[:, m:m + 1]),
                         r=[bank], sr=[dtpb], w=[spl])
                elif _v == "absnb":
                    p.op("scalar", lambda e, m=m, bank=bank, n=ntok:
                         e.activation(spl[:, 0:n], bank[:, 0:n], AF.Abs),
                         r=[bank], w=[spl])
                elif _v == "sqr":
                    p.op("scalar", lambda e, m=m, bank=bank, n=ntok:
                         e.activation(spl[:, 0:n], bank[:, 0:n], AF.Square),
                         r=[bank], w=[spl])
                if _v == "all":
                    p.op("scalar", lambda e, n=ntok: e.activation(spl[:, 0:n], spl[:, 0:n], AF.Exp, scale=-1.0),
                         r=[spl], w=[spl])
                    p.op("scalar", lambda e, n=ntok: e.activation(spl[:, 0:n], spl[:, 0:n], AF.Ln, bias=1.0),
                         r=[spl], w=[spl])
            if cut >= 7:
                p.op("vector", lambda e, m=m, bank=bank, n=ntok:
                     e.tensor_scalar(spr[:, 0:n], bank[:, 0:n], dtpb[:, m:m + 1], 0.0, ALU.add, ALU.max),
                     r=[bank], sr=[dtpb], w=[spr])
                p.op("vector", lambda e, n=ntok: e.tensor_add(spr[:, 0:n], spr[:, 0:n], spl[:, 0:n]),
                     r=[spr, spl], w=[spr])
            else:
                p.op("vector", lambda e, bank=bank, n=ntok: e.tensor_copy(spr[:, 0:n], bank[:, 0:n]),
                     r=[bank], w=[spr])
            p.dio("sync", dt_out[128 * m:128 * (m + 1), c0 - L0H:c0 - L0H + ntok], spr[:, 0:ntok])

    p.finish()
    return p

def host_l0_pre(inputs):
    hid = inputs["L0_attn_hidden"]
    in_w = inputs["L0_attn_in_w"]
    in_maps = []
    wt = in_w.T.reshape(DCH, 128, 32, 128)
    w_in_packed = np.ascontiguousarray(wt.transpose(1, 2, 0, 3).reshape(128, 32 * 1024))
    com = {
        "w_in": w_in_packed,
        "b_in": pcol(inputs["L0_attn_in_b"]),
        "w_lnw": pcol(inputs["L0_attn_ln_w"]),
        "w_lnb": pcol(inputs["L0_attn_ln_b"]),
        "w_cw": np.ascontiguousarray(
            inputs["L0_attn_conv_w"][:, 0, :].reshape(16, 128, 4).transpose(1, 0, 2).reshape(128, 64)),
        "w_cb": pcol(inputs["L0_attn_conv_b"]),
        "w_xp": np.ascontiguousarray(
            inputs["L0_attn_xp_w"].T.reshape(16, 128, 96).transpose(1, 0, 2).reshape(128, 16 * 96)),
        "w_dtp": np.ascontiguousarray(inputs["L0_attn_dtp_w"]).T.copy(),
        "b_dtp": pcol(inputs["L0_attn_dtp_b"]),
    }
    com["w_dtp"] = np.ascontiguousarray(inputs["L0_attn_dtp_w"].T)
    for c in range(NCORES):
        b, s = cm(c)
        t0 = s * SPAN
        span = hid[b, t0:t0 + SPAN]
        if s == 0:
            halo = np.zeros((L0H, D), np.float32)
            hv = 0.0
        else:
            halo = hid[b, t0 - L0H:t0]
            hv = 1.0
        resf = fmj(np.concatenate([halo, span], axis=0))
        m = dict(com)
        m["res"] = resf
        m["hval"] = np.full((128, 1), hv, np.float32)
        in_maps.append(m)
    return in_maps




def build_l0_scan(strict=None, T=8192, TC=2048):
    p = Prog(strict)
    NTC = T // TC

    dt_in = [p.dram(f"dt{b}", [256, T], F32, kind="ExternalInput") for b in range(2)]
    xs_in = [p.dram(f"xs{b}", [256, T], BF16, kind="ExternalInput") for b in range(2)]
    bm_in = [p.dram(f"bm{b}", [16, T], F32, kind="ExternalInput") for b in range(2)]
    cm_in = [p.dram(f"cm{b}", [16, T], F32, kind="ExternalInput") for b in range(2)]
    alog = p.dram("alog", [128, 32], F32, kind="ExternalInput")
    dvec = p.dram("dvec", [128, 2], F32, kind="ExternalInput")
    y_out = [p.dram(f"y{b}", [256, T], BF16, kind="ExternalOutput") for b in range(2)]

    a_sb = p.sb("a_sb", [128, 32], F32)
    d_sb = p.sb("d_sb", [128, 2], F32)
    dtc = [p.sb(f"dtc{i}", [128, TC], F32) for i in range(2)]
    xsc = [p.sb(f"xsc{i}", [128, TC], BF16) for i in range(2)]
    dtx = p.sb("dtx", [128, TC], F32)
    At = p.sb("At", [128, TC], F32)
    Ut = p.sb("Ut", [128, TC], F32)
    st_t = p.sb("st_t", [128, TC], F32)
    yac = p.sb("yac", [128, TC], F32)
    tmp = p.sb("tmp", [128, TC], F32)
    ybf = p.sb("ybf", [128, TC], BF16)
    bbc = [p.sb(f"bbc{i}", [128, TC], F32) for i in range(2)]
    cbc = [p.sb(f"cbc{i}", [128, TC], F32) for i in range(2)]
    st_sb = [p.sb(f"st_sb{dc}", [128, 16], F32) for dc in range(2)]

    p.dio("sync", a_sb[:], alog[:])
    p.dio("sync", d_sb[:], dvec[:])
    p.op("scalar", lambda e: e.activation(a_sb[:], a_sb[:], AF.Exp), r=[a_sb], w=[a_sb])
    p.op("vector", lambda e: e.tensor_scalar_mul(a_sb[:], a_sb[:], -1.0), r=[a_sb], w=[a_sb])

    for b in range(2):
        for dc in range(2):
            p.op("vector", lambda e, dc=dc: e.memset(st_sb[dc][:], 0.0), w=[st_sb[dc]])
        for tc in range(NTC):
            c0 = tc * TC
            for dc in range(2):
                dbuf = dtc[dc % 2]
                xbuf = xsc[dc % 2]
                p.dio("sync", dbuf[:], dt_in[b][128 * dc:128 * (dc + 1), c0:c0 + TC], tag=f"dt{b}.{tc}.{dc}")
                p.dio("sync", xbuf[:], xs_in[b][128 * dc:128 * (dc + 1), c0:c0 + TC], tag=f"xs{b}.{tc}.{dc}")
                p.op("vector", lambda e, d=dbuf, x=xbuf: e.tensor_mul(dtx[:], d[:], x[:]),
                     r=[dbuf, xbuf], w=[dtx])
                p.op("vector", lambda e, x=xbuf, dc=dc: e.tensor_scalar_mul(yac[:], x[:], d_sb[:, dc:dc + 1]),
                     r=[xbuf], sr=[d_sb], w=[yac])
                for n in range(16):
                    bb = bbc[n % 2]; cb = cbc[n % 2]
                    p.dio("gpsimd", bb[:], bm_in[b][n:n + 1, c0:c0 + TC].partition_broadcast(128),
                          tag=f"bbc{b}.{tc}.{dc}.{n}")
                    p.dio("gpsimd", cb[:], cm_in[b][n:n + 1, c0:c0 + TC].partition_broadcast(128),
                          tag=f"cbc{b}.{tc}.{dc}.{n}")
                    p.op("scalar", lambda e, d=dbuf, dc=dc, n=n:
                         e.activation(At[:], d[:], AF.Exp, scale=a_sb[:, dc * 16 + n:dc * 16 + n + 1]),
                         r=[dbuf], sr=[a_sb], w=[At], tag=f"A{b}.{tc}.{dc}.{n}")
                    p.op("vector", lambda e, bb=bb: e.tensor_mul(Ut[:], dtx[:], bb[:]),
                         r=[dtx, bb], w=[Ut], tag=f"U{b}.{tc}.{dc}.{n}")
                    p.op("vector", lambda e, dc=dc, n=n:
                         e.tensor_tensor_scan(st_t[:], At[:], Ut[:], st_sb[dc][:, n:n + 1],
                                              ALU.mult, ALU.add),
                         r=[At, Ut], sr=[st_sb[dc]], w=[st_t], tag=f"S{b}.{tc}.{dc}.{n}")
                    p.op("vector", lambda e, dc=dc, n=n:
                         e.tensor_copy(st_sb[dc][:, n:n + 1], st_t[:, TC - 1:TC]),
                         r=[st_t], w=[st_sb[dc]], tag=f"C{b}.{tc}.{dc}.{n}")
                    p.op("vector", lambda e, cb=cb: e.tensor_mul(tmp[:], st_t[:], cb[:]),
                         r=[st_t, cb], w=[tmp])
                    p.op("vector", lambda e: e.tensor_add(yac[:], yac[:], tmp[:]),
                         r=[yac, tmp], w=[yac])
                p.op("vector", lambda e: e.tensor_copy(ybf[:], yac[:]), r=[yac], w=[ybf])
                p.dio("sync", y_out[b][128 * dc:128 * (dc + 1), c0:c0 + TC], ybf[:], tag=f"y{b}.{tc}.{dc}")
    p.finish()
    return p

def host_l0_scan(dt_full, xs_full, bc_full, T=8192):
    dt_full = dt_full[:, :T]; xs_full = xs_full[:, :T]; bc_full = bc_full[:, :T]
    in_maps = []
    for c in range(NCORES):
        ch = slice(c * 256, (c + 1) * 256)
        m = {}
        for b in range(2):
            m[f"dt{b}"] = np.ascontiguousarray(dt_full[b, :, ch].T)
            m[f"xs{b}"] = np.ascontiguousarray(xs_full[b, :, ch].T)
            m[f"bm{b}"] = np.ascontiguousarray(bc_full[b, :, 0:16].T)
            m[f"cm{b}"] = np.ascontiguousarray(bc_full[b, :, 16:32].T)
        al = np.asarray(_HOST_CTX["A_log"][c * 256:(c + 1) * 256]).reshape(2, 128, 16)
        m["alog"] = np.ascontiguousarray(al.transpose(1, 0, 2).reshape(128, 32))
        dv = np.asarray(_HOST_CTX["D"][c * 256:(c + 1) * 256]).reshape(2, 128)
        m["dvec"] = np.ascontiguousarray(dv.T)
        in_maps.append(m)
    return in_maps

_HOST_CTX = {}




def build_l0_post(strict=None):
    p = Prog(strict)
    INNER, ICH = 2048, 16

    y_in = p.dram("y_in", [INNER, SPAN], BF16, kind="ExternalInput")
    g_in = p.dram("g_in", [INNER, SPAN], BF16, kind="ExternalInput")
    z_in = p.dram("z_in", [D, SPAN], F32, kind="ExternalInput")
    res_in = p.dram("res_in", [D, SPAN], F32, kind="ExternalInput")
    w_out = p.dram("w_out", [128, DCH * INNER], F32, kind="ExternalInput")
    b_out = p.dram("b_out", [128, DCH], F32, kind="ExternalInput")
    r_out = p.dram("r_out", [D, SPAN], F32, kind="ExternalOutput")

    w_bf = p.dram("w_bf", [128, DCH * INNER], BF16)

    yg = [p.sb(f"yg{i}", [128, TT], BF16) for i in range(ICH)]
    yt = p.sb("yt", [128, TT], BF16)
    gt = p.sb("gt", [128, TT], BF16)
    zt = p.sb("zt", [128, TT], F32)
    rt = p.sb("rt", [128, TT], F32)
    ot = p.sb("ot", [128, TT], F32)
    ob = p.sb("ob", [128, DCH], F32)
    wblk = [p.sb(f"wblk{i}", [128, INNER], BF16) for i in range(2)]
    ps_mm = [p.psum(f"ps_mm{i}", [128, TT], F32) for i in range(2)]

    p.dio("sync", ob[:], b_out[:])
    p.dio("gpsimd", w_bf[:], w_out[:])

    for ti in range(NTT):
        c0 = ti * TT
        for i in range(ICH):
            p.dio("sync", yt[:], y_in[128 * i:128 * (i + 1), c0:c0 + TT], tag=f"y{ti}.{i}")
            p.dio("sync", gt[:], g_in[128 * i:128 * (i + 1), c0:c0 + TT], tag=f"g{ti}.{i}")
            p.op("vector", lambda e, i=i: e.tensor_mul(yg[i][:], yt[:], gt[:]),
                 r=[yt, gt], w=[yg[i]])
        for m in range(DCH):
            wb = wblk[m % 2]
            p.dio("sync", wb[:], w_bf[:, m * INNER:(m + 1) * INNER], tag=f"w{m}")
            bank = ps_mm[m % 2]
            for k in range(ICH):
                p.op("tensor", lambda e, k=k, wb=wb, bank=bank:
                     e.matmul(bank[:], wb[:, k * 128:(k + 1) * 128], yg[k][:],
                              start=(k == 0), stop=(k == ICH - 1)),
                     r=[wb, yg[k]], w=[bank])
            p.dio("sync", zt[:], z_in[128 * m:128 * (m + 1), c0:c0 + TT], tag=f"z{ti}.{m}")
            p.dio("sync", rt[:], res_in[128 * m:128 * (m + 1), c0:c0 + TT], tag=f"r{ti}.{m}")
            p.op("vector", lambda e, m=m, bank=bank:
                 e.scalar_tensor_tensor(ot[:], bank[:], ob[:, m:m + 1], zt[:], ALU.add, ALU.add),
                 r=[bank, zt], sr=[ob], w=[ot])
            p.op("vector", lambda e: e.tensor_add(ot[:], ot[:], rt[:]), r=[ot, rt], w=[ot])
            p.dio("sync", r_out[128 * m:128 * (m + 1), c0:c0 + TT], ot[:], tag=f"o{ti}.{m}")
    p.finish()
    return p

def host_l0_post(y_full, gate_full, z_full, hid):
    out_w = _HOST_CTX["out_w"]
    wt = out_w.T.reshape(ICH0 := 16, 128, 8, 128)
    wpack = np.ascontiguousarray(wt.transpose(1, 2, 0, 3).reshape(128, 8 * 2048))
    in_maps = []
    for c in range(NCORES):
        b, s = cm(c)
        sl = slice(s * SPAN, (s + 1) * SPAN)
        in_maps.append({
            "y_in": np.ascontiguousarray(y_full[b, sl].T),
            "g_in": np.ascontiguousarray(gate_full[b, sl].T),
            "z_in": np.ascontiguousarray(z_full[b, sl].T),
            "res_in": np.ascontiguousarray(hid[b, sl].T),
            "w_out": wpack,
            "b_out": pcol(_HOST_CTX["out_b"]),
        })
    return in_maps




def build_l0_ffn(strict=None):
    p = Prog(strict)
    INNER, ICH = 2048, 16
    NE = 8

    res_in = p.dram("res_in", [D, SPAN], F32, kind="ExternalInput")
    w_rw = p.dram("w_rw", [128, DCH * 8], F32, kind="ExternalInput")
    b_rb = p.dram("b_rb", [8, 1], F32, kind="ExternalInput")
    ident_in = p.dram("ident_in", [128, 128], F32, kind="ExternalInput")
    w1_in = p.dram("w1_in", [128, 9 * 16 * 1024], F32, kind="ExternalInput")
    w2_in = p.dram("w2_in", [128, 9 * 16 * 1024], F32, kind="ExternalInput")
    w3_in = p.dram("w3_in", [128, 9 * 8 * 2048], F32, kind="ExternalInput")
    r_out = p.dram("r_out", [D, SPAN], F32, kind="ExternalOutput")

    w1_bf = p.dram("w1_bf", [128, 9 * 16 * 1024], BF16)
    w2_bf = p.dram("w2_bf", [128, 9 * 16 * 1024], BF16)
    w3_bf = p.dram("w3_bf", [128, 9 * 8 * 2048], BF16)
    gates_hbm = p.dram("gates_hbm", [8, SPAN], F32)

    rest = [p.sb(f"rest{i}", [128, TT], F32) for i in range(DCH)]
    x2 = [p.sb(f"x2_{i}", [128, SPAN], BF16) for i in range(DCH)]
    hb = [p.sb(f"hb{i}", [128, TT], BF16) for i in range(ICH)]
    racc = [p.sb(f"racc{i}", [128, SPAN], F32) for i in range(DCH)]
    sq2 = [p.sb(f"sq{i}", [128, TT], BF16) for i in range(2)]
    rowset = [{k: p.sb(f"r0_{k}", [1, TT], F32) for k in ("srow", "qrow", "rstd", "negmur")}]
    onec = p.sb("onec", [128, 1], BF16)
    oner = p.sb("oner", [1, 128], F32)
    rwt = p.sb("rwt", [128, DCH * 8], BF16)
    rbt = p.sb("rbt", [8, 1], F32)
    ident = p.sb("ident", [128, 128], BF16)
    lgt = p.sb("lgt", [8, TT], BF16)
    prt = p.sb("prt", [128, 8], F32)
    m1c = p.sb("m1c", [128, 1], F32)
    m2c = p.sb("m2c", [128, 1], F32)
    eq1 = p.sb("eq1", [128, 8], F32)
    eq2 = p.sb("eq2", [128, 8], F32)
    gtm = p.sb("gtm", [128, 8], BF16)
    gfm = p.sb("gfm", [8, SPAN], F32)
    gbc = p.sb("gbc", [128, TT], F32)
    wbl = [p.sb(f"wbl{i}", [128, 1024], BF16) for i in range(2)]
    wbl3 = [p.sb(f"wbl3{i}", [128, 2048], BF16) for i in range(2)]
    gtile = p.sb("gtile", [128, TT], BF16)
    ot = p.sb("ot", [128, TT], F32)

    ps_s = p.psum("ps_s", [1, TT], F32)
    ps_q = p.psum("ps_q", [1, TT], F32)
    ps_r = p.psum("ps_r", [128, TT], F32)
    ps_m = p.psum("ps_m", [128, TT], F32)
    ps_g = p.psum("ps_g", [128, TT], F32)
    ps_u = p.psum("ps_u", [128, TT], F32)
    ps_l = p.psum("ps_l", [8, TT], F32)
    ps_t = p.psum("ps_t", [128, 128], BF16)

    scr = {"sq": sq2, "ps_s": ps_s, "ps_q": ps_q, "onec": onec,
           "oner": oner, "ps_r": ps_r, "ps_m": ps_m}

    p.dio("gpsimd", rwt[:], w_rw[:])
    p.dio("sync", rbt[:], b_rb[:])
    p.dio("gpsimd", ident[:], ident_in[:])
    p.dio("gpsimd", w1_bf[:], w1_in[:])
    p.dio("gpsimd", w2_bf[:], w2_in[:])
    p.dio("gpsimd", w3_bf[:], w3_in[:])
    p.op("vector", lambda e: e.memset(onec[:], 1.0), w=[onec])
    p.op("vector", lambda e: e.memset(oner[:], 1.0), w=[oner])

    for ti in range(NTT):
        c0 = ti * TT
        for i in range(DCH):
            p.dio("sync", rest[i][:], res_in[128 * i:128 * (i + 1), c0:c0 + TT], tag=f"res{ti}.{i}")
        for i in range(DCH):
            p.op("vector", lambda e, i=i, c0=c0: e.tensor_copy(x2[i][:, c0:c0 + TT], rest[i][:]),
                 r=[rest[i]], w=[x2[i]])
        e_ln(p, f"ln{ti}", [x2[i][:, c0:c0 + TT] for i in range(DCH)], TT, rowset[0], scr)
        e_bc2(p, rowset[0], scr, TT)
        for i in range(DCH):
            p.op("vector", lambda e, i=i, c0=c0: e.tensor_mul(sq2[0][:, 0:TT], rest[i][:], ps_r[:, 0:TT]),
                 r=[rest[i], ps_r], w=[sq2[0]])
            p.op("vector", lambda e, i=i, c0=c0: e.tensor_add(x2[i][:, c0:c0 + TT], sq2[0][:, 0:TT], ps_m[:, 0:TT]),
                 r=[sq2[0], ps_m], w=[x2[i]])
        for k in range(DCH):
            p.op("tensor", lambda e, k=k, c0=c0:
                 e.matmul(ps_l[:, 0:TT], rwt[:, k * 8:(k + 1) * 8], x2[k][:, c0:c0 + TT],
                          start=(k == 0), stop=(k == DCH - 1)),
                 r=[rwt, x2[k]], w=[ps_l])
        p.op("scalar", lambda e: e.add(lgt[:, 0:TT], ps_l[0:8, 0:TT], rbt[0:8, 0:1]),
             r=[ps_l], sr=[rbt], w=[lgt])
        for st in range(TT // 128):
            p.op("tensor", lambda e, st=st: e.transpose(ps_t[:, 0:8], lgt[0:8, st * 128:(st + 1) * 128], ident[0:8, 0:8]),
                 r=[lgt, ident], w=[ps_t])
            p.op("vector", lambda e: e.tensor_copy(prt[:, 0:8], ps_t[:, 0:8]), r=[ps_t], w=[prt])
            p.op("vector", lambda e: e.reduce_max(m1c[:, 0:1], prt[:, 0:8], axis=AX.X), r=[prt], w=[m1c])
            p.op("vector", lambda e: e.tensor_scalar_mul(m1c[:, 0:1], m1c[:, 0:1], -1.0), r=[m1c], w=[m1c])
            p.op("scalar", lambda e: e.activation(prt[:, 0:8], prt[:, 0:8], AF.Exp, bias=m1c[:, 0:1]),
                 r=[prt], sr=[m1c], w=[prt])
            p.op("vector", lambda e: e.reduce_sum(m2c[:, 0:1], prt[:, 0:8], axis=AX.X), r=[prt], w=[m2c])
            p.op("vector", lambda e: e.reciprocal(m2c[:, 0:1], m2c[:, 0:1]), r=[m2c], w=[m2c])
            p.op("vector", lambda e: e.tensor_scalar_mul(prt[:, 0:8], prt[:, 0:8], m2c[:, 0:1]),
                 r=[prt], sr=[m2c], w=[prt])
            p.op("vector", lambda e: e.reduce_max(m1c[:, 0:1], prt[:, 0:8], axis=AX.X), r=[prt], w=[m1c])
            p.op("vector", lambda e: e.tensor_single_scalar(eq1[:, 0:8], prt[:, 0:8], m1c[:, 0:1], ALU.is_equal),
                 r=[prt], sr=[m1c], w=[eq1])
            p.op("vector", lambda e: e.scalar_tensor_tensor(eq2[:, 0:8], eq1[:, 0:8], -2.0, prt[:, 0:8],
                                                            ALU.mult, ALU.add), r=[eq1, prt], w=[eq2])
            p.op("vector", lambda e: e.reduce_max(m2c[:, 0:1], eq2[:, 0:8], axis=AX.X), r=[eq2], w=[m2c])
            p.op("vector", lambda e: e.tensor_single_scalar(eq2[:, 0:8], eq2[:, 0:8], m2c[:, 0:1], ALU.is_equal),
                 r=[eq2], sr=[m2c], w=[eq2])
            p.op("vector", lambda e: e.tensor_scalar_mul(eq1[:, 0:8], eq1[:, 0:8], m1c[:, 0:1]),
                 r=[eq1], sr=[m1c], w=[eq1])
            p.op("vector", lambda e: e.tensor_scalar_mul(eq2[:, 0:8], eq2[:, 0:8], m2c[:, 0:1]),
                 r=[eq2], sr=[m2c], w=[eq2])
            p.op("vector", lambda e: e.tensor_add(eq1[:, 0:8], eq1[:, 0:8], eq2[:, 0:8]), r=[eq1, eq2], w=[eq1])
            p.op("vector", lambda e: e.tensor_add(m1c[:, 0:1], m1c[:, 0:1], m2c[:, 0:1]), r=[m1c, m2c], w=[m1c])
            p.op("vector", lambda e: e.reciprocal(m2c[:, 0:1], m1c[:, 0:1]), r=[m1c], w=[m2c])
            p.op("vector", lambda e: e.tensor_scalar_mul(eq1[:, 0:8], eq1[:, 0:8], m2c[:, 0:1]),
                 r=[eq1], sr=[m2c], w=[eq1])
            p.op("vector", lambda e: e.tensor_copy(gtm[:, 0:8], eq1[:, 0:8]), r=[eq1], w=[gtm])
            p.op("tensor", lambda e: e.transpose(ps_t[0:8, 0:128], gtm[:, 0:8], ident[:]),
                 r=[gtm, ident], w=[ps_t])
            p.op("vector", lambda e, st=st, c0=c0: e.tensor_copy(gfm[0:8, c0 + st * 128:c0 + (st + 1) * 128],
                                                                 ps_t[0:8, 0:128]),
                 r=[ps_t], w=[gfm])
    p.dio("sync", gates_hbm[:], gfm[:])

    for u in range(9):
        for ti in range(NTT):
            c0 = ti * TT
            for m in range(ICH):
                wg = wbl[0]; wu = wbl[1]
                p.dio("sync", wg[:], w1_bf[:, (u * 16 + m) * 1024:(u * 16 + m + 1) * 1024], tag=f"wg")
                p.dio("sync", wu[:], w2_bf[:, (u * 16 + m) * 1024:(u * 16 + m + 1) * 1024], tag=f"wu")
                for k in range(DCH):
                    p.op("tensor", lambda e, k=k, wg=wg, c0=c0:
                         e.matmul(ps_g[:, 0:TT], wg[:, k * 128:(k + 1) * 128], x2[k][:, c0:c0 + TT],
                                  start=(k == 0), stop=(k == DCH - 1)),
                         r=[wg, x2[k]], w=[ps_g])
                for k in range(DCH):
                    p.op("tensor", lambda e, k=k, wu=wu, c0=c0:
                         e.matmul(ps_u[:, 0:TT], wu[:, k * 128:(k + 1) * 128], x2[k][:, c0:c0 + TT],
                                  start=(k == 0), stop=(k == DCH - 1)),
                         r=[wu, x2[k]], w=[ps_u])
                p.op("scalar", lambda e: e.activation(gtile[:, 0:TT], ps_g[:, 0:TT], AF.Silu),
                     r=[ps_g], w=[gtile])
                p.op("vector", lambda e, m=m: e.tensor_mul(hb[m][:], gtile[:, 0:TT], ps_u[:, 0:TT]),
                     r=[gtile, ps_u], w=[hb[m]])
            for m in range(DCH):
                w3 = wbl3[m % 2]
                p.dio("sync", w3[:], w3_bf[:, (u * 8 + m) * 2048:(u * 8 + m + 1) * 2048], tag=f"w3")
                for k in range(ICH):
                    p.op("tensor", lambda e, k=k, w3=w3:
                         e.matmul(ps_g[:, 0:TT], w3[:, k * 128:(k + 1) * 128], hb[k][:],
                                  start=(k == 0), stop=(k == ICH - 1)),
                         r=[w3, hb[k]], w=[ps_g])
                if u < 8:
                    p.dio("gpsimd", gbc[:, 0:TT], gates_hbm[u:u + 1, c0:c0 + TT].partition_broadcast(128),
                          tag=f"gbc")
                    p.op("vector", lambda e: e.tensor_mul(ot[:, 0:TT], ps_g[:, 0:TT], gbc[:, 0:TT]),
                         r=[ps_g, gbc], w=[ot])
                else:
                    p.op("vector", lambda e: e.tensor_copy(ot[:, 0:TT], ps_g[:, 0:TT]), r=[ps_g], w=[ot])
                if u == 0:
                    p.op("vector", lambda e, m=m, c0=c0: e.tensor_copy(racc[m][:, c0:c0 + TT], ot[:, 0:TT]),
                         r=[ot], w=[racc[m]])
                else:
                    p.op("vector", lambda e, m=m, c0=c0: e.tensor_add(racc[m][:, c0:c0 + TT],
                                                                      racc[m][:, c0:c0 + TT], ot[:, 0:TT]),
                         r=[racc[m], ot], w=[racc[m]])
    for ti in range(NTT):
        c0 = ti * TT
        for m in range(DCH):
            p.dio("sync", rest[m][:], res_in[128 * m:128 * (m + 1), c0:c0 + TT], tag=f"res2.{ti}.{m}")
            p.op("vector", lambda e, m=m, c0=c0: e.tensor_add(rest[m][:], rest[m][:], racc[m][:, c0:c0 + TT]),
                 r=[rest[m], racc[m]], w=[rest[m]])
            p.dio("sync", r_out[128 * m:128 * (m + 1), c0:c0 + TT], rest[m][:], tag=f"ro{ti}.{m}")
    p.finish()
    return p

def _pack_lhsT(w, kch, mch):
    K_, M_ = w.shape
    assert K_ == kch * 128 and M_ == mch * 128
    wt = np.asarray(w).reshape(kch, 128, mch, 128)
    return np.ascontiguousarray(wt.transpose(1, 2, 0, 3).reshape(128, mch * kch * 128))

def host_l0_ffn(res1_full, inputs):
    w1 = np.concatenate([_pack_lhsT(inputs["L0_ffn_ew1"][e], 8, 16) for e in range(8)]
                        + [_pack_lhsT(inputs["L0_ffn_sw1"], 8, 16)], axis=1)
    w2 = np.concatenate([_pack_lhsT(inputs["L0_ffn_ew2"][e], 8, 16) for e in range(8)]
                        + [_pack_lhsT(inputs["L0_ffn_sw2"], 8, 16)], axis=1)
    w3 = np.concatenate([_pack_lhsT(inputs["L0_ffn_ew3"][e], 16, 8) for e in range(8)]
                        + [_pack_lhsT(inputs["L0_ffn_sw3"], 16, 8)], axis=1)
    rw = np.ascontiguousarray(inputs["L0_ffn_rw"].reshape(DCH, 128, 8).transpose(1, 0, 2).reshape(128, DCH * 8))
    com = {"w1_in": w1, "w2_in": w2, "w3_in": w3, "w_rw": rw,
           "b_rb": np.ascontiguousarray(inputs["L0_ffn_rb"].reshape(8, 1)),
           "ident_in": np.eye(128, dtype=np.float32)}
    in_maps = []
    for c in range(NCORES):
        b, s = cm(c)
        sl = slice(s * SPAN, (s + 1) * SPAN)
        m = dict(com)
        m["res_in"] = np.ascontiguousarray(res1_full[b, sl].T)
        in_maps.append(m)
    return in_maps




def build_l1_proj(strict=None):
    p = Prog(strict)
    res_in = p.dram("res_in", [D, SPAN], F32, kind="ExternalInput")
    w_all = p.dram("w_all", [128, 4 * 4 * 1024], F32, kind="ExternalInput")
    b_g = p.dram("b_g", [128, 4], F32, kind="ExternalInput")
    outs = {nm: p.dram(f"{nm}_out", [512, SPAN], BF16, kind="ExternalOutput")
            for nm in ("q", "k", "v", "g")}
    w_bf = p.dram("w_bf", [128, 4 * 4 * 1024], BF16)

    rest = [p.sb(f"rest{i}", [128, TT], F32) for i in range(DCH)]
    x3 = [p.sb(f"x3_{i}", [128, TT], BF16) for i in range(DCH)]
    sq2 = [p.sb(f"sq{i}", [128, TT], BF16) for i in range(2)]
    rowset = {k: p.sb(f"r_{k}", [1, TT], F32) for k in ("srow", "qrow", "rstd", "negmur")}
    onec = p.sb("onec", [128, 1], BF16)
    oner = p.sb("oner", [1, 128], F32)
    gb = p.sb("gb", [128, 4], F32)
    wblk = [p.sb(f"wblk{i}", [128, 1024], BF16) for i in range(2)]
    t1 = p.sb("t1", [128, TT], F32)
    t2 = p.sb("t2", [128, TT], F32)
    ob = p.sb("ob", [128, TT], BF16)
    ps_s = p.psum("ps_s", [1, TT], F32)
    ps_q = p.psum("ps_q", [1, TT], F32)
    ps_r = p.psum("ps_r", [128, TT], F32)
    ps_m = p.psum("ps_m", [128, TT], F32)
    ps_mm = [p.psum(f"ps_mm{i}", [128, TT], F32) for i in range(2)]
    scr = {"sq": sq2, "ps_s": ps_s, "ps_q": ps_q, "onec": onec,
           "oner": oner, "ps_r": ps_r, "ps_m": ps_m}

    p.dio("sync", gb[:], b_g[:])
    p.dio("gpsimd", w_bf[:], w_all[:])
    p.op("vector", lambda e: e.memset(onec[:], 1.0), w=[onec])
    p.op("vector", lambda e: e.memset(oner[:], 1.0), w=[oner])

    for ti in range(NTT):
        c0 = ti * TT
        for i in range(DCH):
            p.dio("sync", rest[i][:], res_in[128 * i:128 * (i + 1), c0:c0 + TT], tag=f"res{ti}.{i}")
            p.op("vector", lambda e, i=i: e.tensor_copy(x3[i][:], rest[i][:]), r=[rest[i]], w=[x3[i]])
        e_ln(p, f"ln{ti}", x3, TT, rowset, scr)
        e_bc2(p, rowset, scr, TT)
        for i in range(DCH):
            p.op("vector", lambda e, i=i: e.tensor_mul(t1[:], rest[i][:], ps_r[:, 0:TT]),
                 r=[rest[i], ps_r], w=[t1])
            p.op("vector", lambda e, i=i: e.tensor_add(x3[i][:], t1[:], ps_m[:, 0:TT]),
                 r=[t1, ps_m], w=[x3[i]])
        for wi, nm in enumerate(("q", "k", "v", "g")):
            for m in range(4):
                wb = wblk[m % 2]
                p.dio("sync", wb[:], w_bf[:, (wi * 4 + m) * 1024:(wi * 4 + m + 1) * 1024], tag=f"w{nm}{m}")
                bank = ps_mm[m % 2]
                for k in range(DCH):
                    p.op("tensor", lambda e, k=k, wb=wb, bank=bank:
                         e.matmul(bank[:, 0:TT], wb[:, k * 128:(k + 1) * 128], x3[k][:],
                                  start=(k == 0), stop=(k == DCH - 1)),
                         r=[wb, x3[k]], w=[bank])
                if nm in ("q", "k"):
                    p.op("vector", lambda e, bank=bank: e.tensor_scalar_min(t1[:], bank[:, 0:TT], 0.0),
                         r=[bank], w=[t1])
                    p.op("scalar", lambda e: e.activation(t1[:], t1[:], AF.Exp), r=[t1], w=[t1])
                    p.op("vector", lambda e, bank=bank: e.tensor_scalar_max(t2[:], bank[:, 0:TT], 0.0),
                         r=[bank], w=[t2])
                    p.op("vector", lambda e: e.tensor_add(ob[:], t1[:], t2[:]), r=[t1, t2], w=[ob])
                elif nm == "v":
                    p.op("vector", lambda e, bank=bank: e.tensor_copy(ob[:], bank[:, 0:TT]),
                         r=[bank], w=[ob])
                else:
                    p.op("scalar", lambda e, bank=bank, m=m:
                         e.activation(ob[:], bank[:, 0:TT], AF.Sigmoid, bias=gb[:, m:m + 1]),
                         r=[bank], sr=[gb], w=[ob])
                p.dio("sync", outs[nm][128 * m:128 * (m + 1), c0:c0 + TT], ob[:], tag=f"o{nm}{m}")
    p.finish()
    return p

def host_l1_proj(res2_full, inputs):
    packs = [_pack_lhsT(inputs["L1_attn_W_Q"].T, 8, 4), _pack_lhsT(inputs["L1_attn_W_K"].T, 8, 4),
             _pack_lhsT(inputs["L1_attn_W_V"].T, 8, 4), _pack_lhsT(inputs["L1_attn_W_g_w"].T, 8, 4)]
    com = {"w_all": np.concatenate(packs, axis=1), "b_g": pcol(inputs["L1_attn_W_g_b"])}
    in_maps = []
    for c in range(NCORES):
        b, s = cm(c)
        sl = slice(s * SPAN, (s + 1) * SPAN)
        m = dict(com)
        m["res_in"] = np.ascontiguousarray(res2_full[b, sl].T)
        in_maps.append(m)
    return in_maps


def build_l1_attn(strict=None, T=8192):
    p = Prog(strict)
    Q = 128
    NCH = T // Q
    EPS = 1e-6

    q_in = p.dram("q_in", [2 * 64, T], BF16, kind="ExternalInput")
    k_in = p.dram("k_in", [2 * 64, T], BF16, kind="ExternalInput")
    v_in = p.dram("v_in", [2 * 64, T], BF16, kind="ExternalInput")
    g_in = p.dram("g_in", [2 * 64, T], BF16, kind="ExternalInput")
    dl_in = p.dram("dl_in", [1, 2], F32, kind="ExternalInput")
    tjt_in = p.dram("tjt_in", [128, 128], F32, kind="ExternalInput")
    trit_in = p.dram("trit_in", [128, 128], F32, kind="ExternalInput")
    qp1_in = p.dram("qp1_in", [1, 128], F32, kind="ExternalInput")
    qrev_in = p.dram("qrev_in", [128, 1], F32, kind="ExternalInput")
    qq_in = p.dram("qq_in", [128, 1], F32, kind="ExternalInput")
    ident_in = p.dram("ident_in", [128, 128], F32, kind="ExternalInput")
    y_out = p.dram("y_out", [2 * 64, T], BF16, kind="ExternalOutput")

    qs = p.sb("qs", [64, 1024], BF16)
    ks = p.sb("ks", [64, 1024], BF16)
    vs = p.sb("vs", [64, 1024], BF16)
    gs = p.sb("gs", [64, 1024], BF16)
    ys = p.sb("ys", [64, 1024], BF16)
    dl = p.sb("dl", [1, 2], F32)
    t11 = p.sb("t11", [1, 2], F32)
    t12 = p.sb("t12", [1, 2], F32)
    lng = p.sb("lng", [1, 2], F32)
    lngc = p.sb("lngc", [128, 1], F32)
    tjt = p.sb("tjt", [128, 128], F32)
    trit = p.sb("trit", [128, 128], F32)
    qp1 = p.sb("qp1", [1, 128], F32)
    qrev = p.sb("qrev", [128, 1], F32)
    qqc = p.sb("qqc", [128, 1], F32)
    ident = p.sb("ident", [128, 128], BF16)
    oner = p.sb("oner", [1, 128], F32)
    onec = p.sb("onec", [128, 1], BF16)
    mt = p.sb("mt", [128, 128], F32)
    mtb = p.sb("mtb", [128, 128], BF16)
    gprow = p.sb("gprow", [1, 128], F32)
    gqcol = p.sb("gqcol", [128, 1], F32)
    grc = p.sb("grc", [128, 1], F32)
    gpbc = p.sb("gpbc", [64, 128], F32)
    S_sb = p.sb("S_sb", [64, 64], F32)
    z_sb = p.sb("z_sb", [64, 1], F32)
    Sb = p.sb("Sb", [64, 64], BF16)
    zb = p.sb("zb", [64, 1], BF16)
    vT = p.sb("vT", [128, 64], BF16)
    kdT = p.sb("kdT", [128, 64], BF16)
    att = p.sb("att", [128, 128], BF16)
    qsc = p.sb("qsc", [64, 128], BF16)
    numt = p.sb("numt", [64, 128], F32)
    dent = p.sb("dent", [1, 128], F32)
    yt = p.sb("yt", [64, 128], F32)

    ps_sc = p.psum("ps_sc", [128, 128], F32)
    ps_tr = p.psum("ps_tr", [128, 128], BF16)
    ps_num = p.psum("ps_num", [64, 128], F32)
    ps_den = p.psum("ps_den", [1, 128], F32)
    ps_S = p.psum("ps_S", [64, 64], F32)
    ps_z = p.psum("ps_z", [64, 1], F32)
    ps_bc = p.psum("ps_bc", [64, 128], F32)
    ps_g = p.psum("ps_g", [128, 1], F32)

    for t, s_ in [(dl, dl_in), (tjt, tjt_in), (trit, trit_in), (qp1, qp1_in),
                  (qrev, qrev_in), (qqc, qq_in)]:
        p.dio("sync", t[:], s_[:])
    p.dio("gpsimd", ident[:], ident_in[:])
    p.op("vector", lambda e: e.memset(oner[:], 1.0), w=[oner])
    p.op("vector", lambda e: e.memset(onec[:], 1.0), w=[onec])
    p.op("scalar", lambda e: e.activation(t11[:], dl[:], AF.Abs), r=[dl], w=[t11])
    p.op("scalar", lambda e: e.activation(t11[:], t11[:], AF.Exp, scale=-1.0), r=[t11], w=[t11])
    p.op("scalar", lambda e: e.activation(t11[:], t11[:], AF.Ln, bias=1.0), r=[t11], w=[t11])
    p.op("vector", lambda e: e.tensor_scalar_mul(t12[:], dl[:], -1.0), r=[dl], w=[t12])
    p.op("vector", lambda e: e.tensor_scalar_max(t12[:], t12[:], 0.0), r=[t12], w=[t12])
    p.op("vector", lambda e: e.tensor_add(t12[:], t12[:], t11[:]), r=[t12, t11], w=[t12])
    p.op("vector", lambda e: e.tensor_scalar_mul(lng[:], t12[:], -1.0), r=[t12], w=[lng])

    for u in range(2):
        r0 = u * 64
        p.op("tensor", lambda e, u=u: e.matmul(ps_g[:, 0:1], oner[0:1, :], lng[0:1, u:u + 1],
                                               start=True, stop=True), r=[oner, lng], w=[ps_g])
        p.op("vector", lambda e: e.tensor_copy(lngc[:], ps_g[:, 0:1]), r=[ps_g], w=[lngc])
        p.op("scalar", lambda e: e.activation(mt[:], tjt[:], AF.Exp, scale=lngc[:, 0:1]),
             r=[tjt], sr=[lngc], w=[mt])
        p.op("vector", lambda e: e.tensor_mul(mt[:], mt[:], trit[:]), r=[mt, trit], w=[mt])
        p.op("vector", lambda e: e.tensor_copy(mtb[:], mt[:]), r=[mt], w=[mtb])
        p.op("scalar", lambda e, u=u: e.activation(gprow[0:1, :], qp1[0:1, :], AF.Exp, scale=lng[0:1, u:u + 1]),
             r=[qp1], sr=[lng], w=[gprow])
        p.op("scalar", lambda e: e.activation(grc[:], qrev[:], AF.Exp, scale=lngc[:, 0:1]),
             r=[qrev], sr=[lngc], w=[grc])
        p.op("scalar", lambda e: e.activation(gqcol[:], qqc[:], AF.Exp, scale=lngc[:, 0:1]),
             r=[qqc], sr=[lngc], w=[gqcol])
        p.op("tensor", lambda e: e.matmul(ps_bc[:, 0:128], oner[0:1, 0:64], gprow[0:1, :],
                                          start=True, stop=True), r=[oner, gprow], w=[ps_bc])
        p.op("vector", lambda e: e.tensor_copy(gpbc[:], ps_bc[:, 0:128]), r=[ps_bc], w=[gpbc])
        p.op("vector", lambda e: e.memset(S_sb[:], 0.0), w=[S_sb])
        p.op("vector", lambda e: e.memset(z_sb[:], 0.0), w=[z_sb])

        for c in range(NCH):
            st = c % 8
            if st == 0:
                c0 = c * Q
                p.dio("sync", qs[:], q_in[r0:r0 + 64, c0:c0 + 1024], tag="qs")
                p.dio("sync", ks[:], k_in[r0:r0 + 64, c0:c0 + 1024], tag="ks")
                p.dio("sync", vs[:], v_in[r0:r0 + 64, c0:c0 + 1024], tag="vs")
                p.dio("sync", gs[:], g_in[r0:r0 + 64, c0:c0 + 1024], tag="gs")
            sl = slice(st * Q, (st + 1) * Q)
            p.op("vector", lambda e: e.tensor_copy(Sb[:], S_sb[:]), r=[S_sb], w=[Sb])
            p.op("vector", lambda e: e.tensor_copy(zb[:], z_sb[:]), r=[z_sb], w=[zb])
            p.op("tensor", lambda e, sl=sl: e.transpose(ps_tr[:, 0:64], vs[:, sl], ident[0:64, 0:64]),
                 r=[vs, ident], w=[ps_tr])
            p.op("vector", lambda e: e.tensor_copy(vT[:], ps_tr[:, 0:64]), r=[ps_tr], w=[vT])
            p.op("tensor", lambda e, sl=sl: e.transpose(ps_tr[:, 0:64], ks[:, sl], ident[0:64, 0:64]),
                 r=[ks, ident], w=[ps_tr])
            p.op("vector", lambda e: e.tensor_scalar_mul(kdT[:], ps_tr[:, 0:64], grc[:, 0:1]),
                 r=[ps_tr], sr=[grc], w=[kdT])
            p.op("tensor", lambda e, sl=sl: e.matmul(ps_sc[:], ks[:, sl], qs[:, sl], start=True, stop=True),
                 r=[ks, qs], w=[ps_sc])
            p.op("vector", lambda e: e.tensor_mul(att[:], ps_sc[:], mtb[:]), r=[ps_sc, mtb], w=[att])
            p.op("vector", lambda e, sl=sl: e.tensor_mul(qsc[:], qs[:, sl], gpbc[:]),
                 r=[qs, gpbc], w=[qsc])
            p.op("tensor", lambda e: e.matmul(ps_num[:], vT[:], att[:], start=True, stop=False),
                 r=[vT, att], w=[ps_num])
            p.op("tensor", lambda e: e.matmul(ps_num[:], Sb[:], qsc[:], start=False, stop=True),
                 r=[Sb, qsc], w=[ps_num])
            p.op("tensor", lambda e: e.matmul(ps_den[:], onec[:, 0:1], att[:], start=True, stop=False),
                 r=[onec, att], w=[ps_den])
            p.op("tensor", lambda e: e.matmul(ps_den[:], zb[:], qsc[:], start=False, stop=True),
                 r=[zb, qsc], w=[ps_den])
            p.op("tensor", lambda e: e.matmul(ps_S[:], kdT[:], vT[:], start=True, stop=True),
                 r=[kdT, vT], w=[ps_S])
            p.op("tensor", lambda e: e.matmul(ps_z[:], kdT[:], onec[:, 0:1], start=True, stop=True),
                 r=[kdT, onec], w=[ps_z])
            p.op("vector", lambda e: e.scalar_tensor_tensor(S_sb[:], S_sb[:], gqcol[0:64, 0:1], ps_S[:],
                                                            ALU.mult, ALU.add),
                 r=[S_sb, ps_S], sr=[gqcol], w=[S_sb])
            p.op("vector", lambda e: e.scalar_tensor_tensor(z_sb[:], z_sb[:], gqcol[0:64, 0:1], ps_z[:],
                                                            ALU.mult, ALU.add),
                 r=[z_sb, ps_z], sr=[gqcol], w=[z_sb])
            p.op("vector", lambda e: e.tensor_scalar_add(dent[0:1, :], ps_den[0:1, :], EPS),
                 r=[ps_den], w=[dent])
            p.op("vector", lambda e: e.reciprocal(dent[0:1, :], dent[0:1, :]), r=[dent], w=[dent])
            p.op("tensor", lambda e: e.matmul(ps_bc[:, 0:128], oner[0:1, 0:64], dent[0:1, :],
                                              start=True, stop=True), r=[oner, dent], w=[ps_bc])
            p.op("vector", lambda e: e.tensor_copy(numt[:], ps_num[:]), r=[ps_num], w=[numt])
            p.op("vector", lambda e: e.tensor_mul(yt[:], numt[:], ps_bc[:, 0:128]),
                 r=[numt, ps_bc], w=[yt])
            p.op("vector", lambda e, sl=sl: e.tensor_mul(ys[:, sl], yt[:], gs[:, sl]),
                 r=[yt, gs], w=[ys])
            if st == 7:
                c0 = (c - 7) * Q
                p.dio("sync", y_out[r0:r0 + 64, c0:c0 + 1024], ys[:], tag="ys")
    p.finish()
    return p

def host_l1_attn(qf, kf, vf, gf, inputs, T=8192):
    Q = 128
    tj = np.arange(128)[None, :] - np.arange(128)[:, None]
    com = {
        "tjt_in": np.maximum(tj, 0).astype(np.float32),
        "trit_in": (tj >= 0).astype(np.float32),
        "qp1_in": (np.arange(128)[None, :] + 1).astype(np.float32),
        "qrev_in": (Q - 1 - np.arange(128))[:, None].astype(np.float32),
        "qq_in": np.full((128, 1), float(Q), np.float32),
        "ident_in": np.eye(128, dtype=np.float32),
    }
    dl = inputs["L1_attn_decay_log"]
    in_maps = []
    for c in range(NCORES):
        units = [(c * 2) // 8, ((c * 2) % 8)], [(c * 2 + 1) // 8, ((c * 2 + 1) % 8)]
        m = dict(com)
        for nm, arr in (("q", qf), ("k", kf), ("v", vf), ("g", gf)):
            rows = []
            for b, h in units:
                rows.append(arr[b, :T, 64 * h:64 * (h + 1)].T)
            m[f"{nm}_in"] = np.ascontiguousarray(np.concatenate(rows, axis=0))
        m["dl_in"] = np.array([[dl[h] for b, h in units]], np.float32)
        in_maps.append(m)
    return in_maps


def build_l1_post(strict=None):
    p = Prog(strict)
    KCH = 4
    y_in = p.dram("y_in", [512, SPAN], BF16, kind="ExternalInput")
    res_in = p.dram("res_in", [D, SPAN], F32, kind="ExternalInput")
    w_out = p.dram("w_out", [128, DCH * 512], F32, kind="ExternalInput")
    r_out = p.dram("r_out", [D, SPAN], F32, kind="ExternalOutput")
    w_bf = p.dram("w_bf", [128, DCH * 512], BF16)

    yg = [p.sb(f"yg{i}", [128, TT], BF16) for i in range(KCH)]
    rt = p.sb("rt", [128, TT], F32)
    ot = p.sb("ot", [128, TT], F32)
    wblk = [p.sb(f"wblk{i}", [128, 512], BF16) for i in range(2)]
    ps_mm = [p.psum(f"ps_mm{i}", [128, TT], F32) for i in range(2)]

    p.dio("gpsimd", w_bf[:], w_out[:])
    for ti in range(NTT):
        c0 = ti * TT
        for i in range(KCH):
            p.dio("sync", yg[i][:], y_in[128 * i:128 * (i + 1), c0:c0 + TT], tag=f"y{ti}.{i}")
        for m in range(DCH):
            wb = wblk[m % 2]
            p.dio("sync", wb[:], w_bf[:, m * 512:(m + 1) * 512], tag=f"w{m}")
            bank = ps_mm[m % 2]
            for k in range(KCH):
                p.op("tensor", lambda e, k=k, wb=wb, bank=bank:
                     e.matmul(bank[:], wb[:, k * 128:(k + 1) * 128], yg[k][:],
                              start=(k == 0), stop=(k == KCH - 1)),
                     r=[wb, yg[k]], w=[bank])
            p.dio("sync", rt[:], res_in[128 * m:128 * (m + 1), c0:c0 + TT], tag=f"r{ti}.{m}")
            p.op("vector", lambda e, bank=bank: e.tensor_add(ot[:], bank[:], rt[:]),
                 r=[bank, rt], w=[ot])
            p.dio("sync", r_out[128 * m:128 * (m + 1), c0:c0 + TT], ot[:], tag=f"o{ti}.{m}")
    p.finish()
    return p

def host_l1_post(y_full, res2_full, inputs):
    wpack = _pack_lhsT(inputs["L1_attn_W_O"].T, 4, 8)
    in_maps = []
    for c in range(NCORES):
        b, s = cm(c)
        sl = slice(s * SPAN, (s + 1) * SPAN)
        in_maps.append({
            "y_in": np.ascontiguousarray(y_full[b, sl].T),
            "res_in": np.ascontiguousarray(res2_full[b, sl].T),
            "w_out": wpack,
        })
    return in_maps




def build_l1_ffn(strict=None, dbg=False):
    p = Prog(strict)
    NE = 16
    BIG = 1e30

    res_in = p.dram("res_in", [D, SPAN], F32, kind="ExternalInput")
    w_rw = p.dram("w_rw", [128, DCH * NE], F32, kind="ExternalInput")
    eb_in = p.dram("eb_in", [1, NE], F32, kind="ExternalInput")
    ident_in = p.dram("ident_in", [128, 128], F32, kind="ExternalInput")
    w1_in = p.dram("w1_in", [128, NE * 4 * 1024], F32, kind="ExternalInput")
    w2_in = p.dram("w2_in", [128, NE * 4 * 1024], F32, kind="ExternalInput")
    w3_in = p.dram("w3_in", [128, NE * 8 * 512], F32, kind="ExternalInput")
    sh1_in = p.dram("sh1_in", [128, 8 * 1024], F32, kind="ExternalInput")
    sh2_in = p.dram("sh2_in", [128, 8 * 1024], F32, kind="ExternalInput")
    sh3_in = p.dram("sh3_in", [128, 8 * 1024], F32, kind="ExternalInput")
    r_out = p.dram("r_out", [D, SPAN], F32, kind="ExternalOutput")

    w1_bf = p.dram("w1_bf", [128, NE * 4 * 1024], BF16)
    w2_bf = p.dram("w2_bf", [128, NE * 4 * 1024], BF16)
    w3_bf = p.dram("w3_bf", [128, NE * 8 * 512], BF16)
    sh1_bf = p.dram("sh1_bf", [128, 8 * 1024], BF16)
    sh2_bf = p.dram("sh2_bf", [128, 8 * 1024], BF16)
    sh3_bf = p.dram("sh3_bf", [128, 8 * 1024], BF16)
    gates_hbm = p.dram("gates_hbm", [NE, SPAN], F32, kind=("ExternalOutput" if dbg else "Internal"))

    rest = [p.sb(f"rest{i}", [128, TT], F32) for i in range(DCH)]
    x2 = [p.sb(f"x2_{i}", [128, SPAN], BF16) for i in range(DCH)]
    hb = [p.sb(f"hb{i}", [128, TT], BF16) for i in range(8)]
    racc = [p.sb(f"racc{i}", [128, SPAN], F32) for i in range(DCH)]
    sq2 = [p.sb(f"sq{i}", [128, TT], BF16) for i in range(2)]
    rowset = {k: p.sb(f"r_{k}", [1, TT], F32) for k in ("srow", "qrow", "rstd", "negmur")}
    onec = p.sb("onec", [128, 1], BF16)
    oner = p.sb("oner", [1, 128], F32)
    rwt = p.sb("rwt", [128, DCH * NE], BF16)
    ebr = p.sb("ebr", [1, NE], F32)
    ebt = p.sb("ebt", [128, NE], F32)
    ident = p.sb("ident", [128, 128], BF16)
    lgt = p.sb("lgt", [NE, TT], BF16)
    prt = p.sb("prt", [128, NE], F32)
    sct = p.sb("sct", [128, NE], F32)
    mskd = p.sb("mskd", [128, NE], F32)
    gst = p.sb("gst", [128, 4], F32)
    wrk16 = p.sb("wrk16", [128, NE], F32)
    m1c = p.sb("m1c", [128, 1], F32)
    m2c = p.sb("m2c", [128, 1], F32)
    eqt = p.sb("eqt", [128, NE], F32)
    gtm = p.sb("gtm", [128, NE], BF16)
    gfm = p.sb("gfm", [NE, SPAN], F32)
    gbc = p.sb("gbc", [128, TT], F32)
    wbl = [p.sb(f"wbl{i}", [128, 1024], BF16) for i in range(2)]
    wbl3 = [p.sb(f"wbl3{i}", [128, 1024], BF16) for i in range(2)]
    gtile = p.sb("gtile", [128, TT], BF16)
    ot = p.sb("ot", [128, TT], F32)

    ps_s = p.psum("ps_s", [1, TT], F32)
    ps_q = p.psum("ps_q", [1, TT], F32)
    ps_r = p.psum("ps_r", [128, TT], F32)
    ps_m = p.psum("ps_m", [128, TT], F32)
    ps_g = p.psum("ps_g", [128, TT], F32)
    ps_u = p.psum("ps_u", [128, TT], F32)
    ps_l = p.psum("ps_l", [NE, TT], F32)
    ps_t = p.psum("ps_t", [128, 128], BF16)

    scr = {"sq": sq2, "ps_s": ps_s, "ps_q": ps_q, "onec": onec,
           "oner": oner, "ps_r": ps_r, "ps_m": ps_m}

    p.dio("gpsimd", rwt[:], w_rw[:])
    p.dio("sync", ebr[:], eb_in[:])
    p.dio("gpsimd", ident[:], ident_in[:])
    for dst, srcw in [(w1_bf, w1_in), (w2_bf, w2_in), (w3_bf, w3_in),
                      (sh1_bf, sh1_in), (sh2_bf, sh2_in), (sh3_bf, sh3_in)]:
        p.dio("gpsimd", dst[:], srcw[:])
    p.op("vector", lambda e: e.memset(onec[:], 1.0), w=[onec])
    p.op("vector", lambda e: e.memset(oner[:], 1.0), w=[oner])
    p.op("tensor", lambda e: e.matmul(ps_r[:, 0:NE], oner[0:1, :], ebr[0:1, 0:NE], start=True, stop=True),
         r=[oner, ebr], w=[ps_r])
    p.op("vector", lambda e: e.tensor_copy(ebt[:], ps_r[:, 0:NE]), r=[ps_r], w=[ebt])

    for ti in range(NTT):
        c0 = ti * TT
        for i in range(DCH):
            p.dio("sync", rest[i][:], res_in[128 * i:128 * (i + 1), c0:c0 + TT], tag=f"res{ti}.{i}")
            p.op("vector", lambda e, i=i, c0=c0: e.tensor_copy(x2[i][:, c0:c0 + TT], rest[i][:]),
                 r=[rest[i]], w=[x2[i]])
        e_ln(p, f"ln{ti}", [x2[i][:, c0:c0 + TT] for i in range(DCH)], TT, rowset, scr)
        e_bc2(p, rowset, scr, TT)
        for i in range(DCH):
            p.op("vector", lambda e, i=i: e.tensor_mul(sq2[0][:, 0:TT], rest[i][:], ps_r[:, 0:TT]),
                 r=[rest[i], ps_r], w=[sq2[0]])
            p.op("vector", lambda e, i=i, c0=c0: e.tensor_add(x2[i][:, c0:c0 + TT], sq2[0][:, 0:TT], ps_m[:, 0:TT]),
                 r=[sq2[0], ps_m], w=[x2[i]])
        for k in range(DCH):
            p.op("tensor", lambda e, k=k, c0=c0:
                 e.matmul(ps_l[:, 0:TT], rwt[:, k * NE:(k + 1) * NE], x2[k][:, c0:c0 + TT],
                          start=(k == 0), stop=(k == DCH - 1)),
                 r=[rwt, x2[k]], w=[ps_l])
        p.op("scalar", lambda e: e.activation(lgt[:, 0:TT], ps_l[0:NE, 0:TT], AF.Sigmoid),
             r=[ps_l], w=[lgt])
        for st in range(TT // 128):
            p.op("tensor", lambda e, st=st: e.transpose(ps_t[:, 0:NE], lgt[0:NE, st * 128:(st + 1) * 128],
                                                        ident[0:NE, 0:NE]),
                 r=[lgt, ident], w=[ps_t])
            p.op("vector", lambda e: e.tensor_copy(prt[:], ps_t[:, 0:NE]), r=[ps_t], w=[prt])
            p.op("vector", lambda e: e.tensor_add(sct[:], prt[:], ebt[:]), r=[prt, ebt], w=[sct])
            for g_ in range(4):
                gsl = slice(4 * g_, 4 * g_ + 4)
                p.op("vector", lambda e, gsl=gsl: e.reduce_max(m1c[:, 0:1], sct[:, gsl], axis=AX.X),
                     r=[sct], w=[m1c])
                p.op("vector", lambda e, gsl=gsl: e.tensor_single_scalar(eqt[:, 0:4], sct[:, gsl], m1c[:, 0:1], ALU.is_equal),
                     r=[sct], sr=[m1c], w=[eqt])
                p.op("vector", lambda e, gsl=gsl: e.scalar_tensor_tensor(eqt[:, 0:4], eqt[:, 0:4], -BIG, sct[:, gsl],
                                                                         ALU.mult, ALU.add),
                     r=[eqt, sct], w=[eqt])
                p.op("vector", lambda e: e.reduce_max(m2c[:, 0:1], eqt[:, 0:4], axis=AX.X),
                     r=[eqt], w=[m2c])
                p.op("vector", lambda e, g_=g_: e.tensor_add(gst[:, g_:g_ + 1], m1c[:, 0:1], m2c[:, 0:1]),
                     r=[m1c, m2c], w=[gst])
            p.op("vector", lambda e: e.reduce_max(m1c[:, 0:1], gst[:], axis=AX.X), r=[gst], w=[m1c])
            p.op("vector", lambda e: e.tensor_single_scalar(eqt[:, 0:4], gst[:], m1c[:, 0:1], ALU.is_equal),
                 r=[gst], sr=[m1c], w=[eqt])
            p.op("vector", lambda e: e.scalar_tensor_tensor(eqt[:, 0:4], eqt[:, 0:4], -BIG, gst[:],
                                                            ALU.mult, ALU.add),
                 r=[eqt, gst], w=[eqt])
            p.op("vector", lambda e: e.reduce_max(m2c[:, 0:1], eqt[:, 0:4], axis=AX.X), r=[eqt], w=[m2c])
            p.op("vector", lambda e: e.tensor_single_scalar(eqt[:, 0:4], gst[:], m2c[:, 0:1], ALU.is_ge),
                 r=[gst], sr=[m2c], w=[eqt])
            for g_ in range(4):
                p.op("vector", lambda e, g_=g_: e.tensor_scalar_mul(mskd[:, 4 * g_:4 * g_ + 4],
                                                                    sct[:, 4 * g_:4 * g_ + 4], eqt[:, g_:g_ + 1]),
                     r=[sct], sr=[eqt], w=[mskd])
            wrk = gst
            p.op("vector", lambda e: e.tensor_copy(wrk16[:], mskd[:]), r=[mskd], w=[wrk16])
            for _ in range(3):
                p.op("vector", lambda e: e.reduce_max(m1c[:, 0:1], wrk16[:], axis=AX.X), r=[wrk16], w=[m1c])
                p.op("vector", lambda e: e.tensor_single_scalar(eqt[:], wrk16[:], m1c[:, 0:1], ALU.is_equal),
                     r=[wrk16], sr=[m1c], w=[eqt])
                p.op("vector", lambda e: e.scalar_tensor_tensor(wrk16[:], eqt[:], -BIG, wrk16[:],
                                                                ALU.mult, ALU.add),
                     r=[eqt, wrk16], w=[wrk16])
            p.op("vector", lambda e: e.reduce_max(m1c[:, 0:1], wrk16[:], axis=AX.X), r=[wrk16], w=[m1c])
            p.op("vector", lambda e: e.tensor_single_scalar(eqt[:], mskd[:], m1c[:, 0:1], ALU.is_ge),
                 r=[mskd], sr=[m1c], w=[eqt])
            p.op("vector", lambda e: e.tensor_mul(eqt[:], eqt[:], prt[:]), r=[eqt, prt], w=[eqt])
            p.op("vector", lambda e: e.reduce_sum(m1c[:, 0:1], eqt[:], axis=AX.X), r=[eqt], w=[m1c])
            p.op("vector", lambda e: e.tensor_scalar_add(m1c[:, 0:1], m1c[:, 0:1], 1e-20), r=[m1c], w=[m1c])
            p.op("vector", lambda e: e.reciprocal(m2c[:, 0:1], m1c[:, 0:1]), r=[m1c], w=[m2c])
            p.op("vector", lambda e: e.tensor_scalar_mul(eqt[:], eqt[:], m2c[:, 0:1]), r=[eqt], sr=[m2c], w=[eqt])
            p.op("vector", lambda e: e.tensor_scalar_mul(eqt[:], eqt[:], 2.5), r=[eqt], w=[eqt])
            p.op("vector", lambda e: e.tensor_copy(gtm[:], eqt[:]), r=[eqt], w=[gtm])
            p.op("tensor", lambda e: e.transpose(ps_t[0:NE, 0:128], gtm[:], ident[:]),
                 r=[gtm, ident], w=[ps_t])
            p.op("vector", lambda e, st=st, c0=c0: e.tensor_copy(gfm[0:NE, c0 + st * 128:c0 + (st + 1) * 128],
                                                                 ps_t[0:NE, 0:128]),
                 r=[ps_t], w=[gfm])
    p.dio("sync", gates_hbm[:], gfm[:])

    import os as _os
    _cut = _os.environ.get("L1CUT", "all")
    for u in ([] if _cut == "sh" else range(NE)):
        for ti in range(NTT):
            c0 = ti * TT
            for m in range(4):
                wg = wbl[0]; wu = wbl[1]
                p.dio("sync", wg[:], w1_bf[:, (u * 4 + m) * 1024:(u * 4 + m + 1) * 1024], tag="wg")
                p.dio("sync", wu[:], w2_bf[:, (u * 4 + m) * 1024:(u * 4 + m + 1) * 1024], tag="wu")
                for k in range(DCH):
                    p.op("tensor", lambda e, k=k, wg=wg, c0=c0:
                         e.matmul(ps_g[:, 0:TT], wg[:, k * 128:(k + 1) * 128], x2[k][:, c0:c0 + TT],
                                  start=(k == 0), stop=(k == DCH - 1)),
                         r=[wg, x2[k]], w=[ps_g])
                for k in range(DCH):
                    p.op("tensor", lambda e, k=k, wu=wu, c0=c0:
                         e.matmul(ps_u[:, 0:TT], wu[:, k * 128:(k + 1) * 128], x2[k][:, c0:c0 + TT],
                                  start=(k == 0), stop=(k == DCH - 1)),
                         r=[wu, x2[k]], w=[ps_u])
                p.op("scalar", lambda e: e.activation(gtile[:, 0:TT], ps_g[:, 0:TT], AF.Silu),
                     r=[ps_g], w=[gtile])
                p.op("vector", lambda e, m=m: e.tensor_mul(hb[m][:], gtile[:, 0:TT], ps_u[:, 0:TT]),
                     r=[gtile, ps_u], w=[hb[m]])
            p.dio("gpsimd", gbc[:, 0:TT], gates_hbm[u:u + 1, c0:c0 + TT].partition_broadcast(128),
                  tag="gbc")
            for m in range(DCH):
                w3 = wbl3[m % 2]
                p.dio("sync", w3[:, 0:512], w3_bf[:, (u * 8 + m) * 512:(u * 8 + m + 1) * 512], tag="w3")
                for k in range(4):
                    p.op("tensor", lambda e, k=k, w3=w3:
                         e.matmul(ps_g[:, 0:TT], w3[:, k * 128:(k + 1) * 128], hb[k][:],
                                  start=(k == 0), stop=(k == 3)),
                         r=[w3, hb[k]], w=[ps_g])
                p.op("vector", lambda e: e.tensor_mul(ot[:, 0:TT], ps_g[:, 0:TT], gbc[:, 0:TT]),
                     r=[ps_g, gbc], w=[ot])
                if u == 0:
                    p.op("vector", lambda e, m=m, c0=c0: e.tensor_copy(racc[m][:, c0:c0 + TT], ot[:, 0:TT]),
                         r=[ot], w=[racc[m]])
                else:
                    p.op("vector", lambda e, m=m, c0=c0: e.tensor_add(racc[m][:, c0:c0 + TT],
                                                                      racc[m][:, c0:c0 + TT], ot[:, 0:TT]),
                         r=[racc[m], ot], w=[racc[m]])
    if _cut == "sh":
        for m in range(DCH):
            p.op("vector", lambda e, m=m: e.memset(racc[m][:], 0.0), w=[racc[m]])
    for ti in ([] if _cut == "exp" else range(NTT)):
        c0 = ti * TT
        for m in range(8):
            wg = wbl[0]; wu = wbl[1]
            p.dio("sync", wg[:], sh1_bf[:, m * 1024:(m + 1) * 1024], tag="wg")
            p.dio("sync", wu[:], sh2_bf[:, m * 1024:(m + 1) * 1024], tag="wu")
            for k in range(DCH):
                p.op("tensor", lambda e, k=k, wg=wg, c0=c0:
                     e.matmul(ps_g[:, 0:TT], wg[:, k * 128:(k + 1) * 128], x2[k][:, c0:c0 + TT],
                              start=(k == 0), stop=(k == DCH - 1)),
                     r=[wg, x2[k]], w=[ps_g])
            for k in range(DCH):
                p.op("tensor", lambda e, k=k, wu=wu, c0=c0:
                     e.matmul(ps_u[:, 0:TT], wu[:, k * 128:(k + 1) * 128], x2[k][:, c0:c0 + TT],
                              start=(k == 0), stop=(k == DCH - 1)),
                     r=[wu, x2[k]], w=[ps_u])
            p.op("scalar", lambda e: e.activation(gtile[:, 0:TT], ps_g[:, 0:TT], AF.Silu),
                 r=[ps_g], w=[gtile])
            p.op("vector", lambda e, m=m: e.tensor_mul(hb[m][:], gtile[:, 0:TT], ps_u[:, 0:TT]),
                 r=[gtile, ps_u], w=[hb[m]])
        for m in range(DCH):
            w3 = wbl3[m % 2]
            p.dio("sync", w3[:], sh3_bf[:, m * 1024:(m + 1) * 1024], tag="w3")
            for k in range(8):
                p.op("tensor", lambda e, k=k, w3=w3:
                     e.matmul(ps_g[:, 0:TT], w3[:, k * 128:(k + 1) * 128], hb[k][:],
                              start=(k == 0), stop=(k == 7)),
                     r=[w3, hb[k]], w=[ps_g])
            p.op("vector", lambda e, m=m, c0=c0: e.tensor_add(racc[m][:, c0:c0 + TT],
                                                              racc[m][:, c0:c0 + TT], ps_g[:, 0:TT]),
                 r=[racc[m], ps_g], w=[racc[m]])
    for ti in range(NTT):
        c0 = ti * TT
        for m in range(DCH):
            p.dio("sync", rest[m][:], res_in[128 * m:128 * (m + 1), c0:c0 + TT], tag=f"res2.{ti}.{m}")
            p.op("vector", lambda e, m=m, c0=c0: e.tensor_add(rest[m][:], rest[m][:], racc[m][:, c0:c0 + TT]),
                 r=[rest[m], racc[m]], w=[rest[m]])
            p.dio("sync", r_out[128 * m:128 * (m + 1), c0:c0 + TT], rest[m][:], tag=f"ro{ti}.{m}")
    p.finish()
    return p

def host_l1_ffn(res3_full, inputs):
    NE = 16
    w1 = np.concatenate([_pack_lhsT(inputs["L1_ffn_gate_w"][e].T, 8, 4) for e in range(NE)], axis=1)
    w2 = np.concatenate([_pack_lhsT(inputs["L1_ffn_up_w"][e].T, 8, 4) for e in range(NE)], axis=1)
    w3 = np.concatenate([_pack_lhsT(inputs["L1_ffn_down_w"][e].T, 4, 8) for e in range(NE)], axis=1)
    com = {
        "w1_in": w1, "w2_in": w2, "w3_in": w3,
        "sh1_in": _pack_lhsT(inputs["L1_ffn_sh_gate_w"].T, 8, 8),
        "sh2_in": _pack_lhsT(inputs["L1_ffn_sh_up_w"].T, 8, 8),
        "sh3_in": _pack_lhsT(inputs["L1_ffn_sh_down_w"].T, 8, 8),
        "w_rw": np.ascontiguousarray(inputs["L1_ffn_router_weight"].T.reshape(DCH, 128, NE)
                                     .transpose(1, 0, 2).reshape(128, DCH * NE)),
        "eb_in": np.ascontiguousarray(inputs["L1_ffn_e_bias"].reshape(1, NE)),
        "ident_in": np.eye(128, dtype=np.float32),
    }
    in_maps = []
    for c in range(NCORES):
        b, s = cm(c)
        sl = slice(s * SPAN, (s + 1) * SPAN)
        m = dict(com)
        m["res_in"] = np.ascontiguousarray(res3_full[b, sl].T)
        in_maps.append(m)
    return in_maps




def build_l2_pre(strict=None):
    p = Prog(strict)
    res_in = p.dram("res_in", [D, SPAN], F32, kind="ExternalInput")
    wq_in = p.dram("wq_in", [128, DCH * 1024], F32, kind="ExternalInput")
    wk_in = p.dram("wk_in", [128, DCH * 1024], F32, kind="ExternalInput")
    wv_in = p.dram("wv_in", [128, DCH * 1024], F32, kind="ExternalInput")
    wrq_in = p.dram("wrq_in", [128, DCH * 256], F32, kind="ExternalInput")
    bq_in = p.dram("bq_in", [128, 8], F32, kind="ExternalInput")
    bk_in = p.dram("bk_in", [128, 8], F32, kind="ExternalInput")
    bv_in = p.dram("bv_in", [128, 1024], F32, kind="ExternalInput")
    brq_in = p.dram("brq_in", [128, 2], F32, kind="ExternalInput")
    q_out = p.dram("q_out", [D, SPAN], BF16, kind="ExternalOutput")
    k_out = p.dram("k_out", [D, SPAN], BF16, kind="ExternalOutput")
    v_out = p.dram("v_out", [SPAN, D], BF16, kind="ExternalOutput")
    rq_out = p.dram("rq_out", [256, SPAN], BF16, kind="ExternalOutput")
    pool_out = p.dram("pool_out", [D, 16], F32, kind="ExternalOutput")

    wq_bf = p.dram("wq_bf", [128, DCH * 1024], BF16)
    wk_bf = p.dram("wk_bf", [128, DCH * 1024], BF16)
    wv_bf = p.dram("wv_bf", [128, DCH * 1024], BF16)
    wrq_bf = p.dram("wrq_bf", [128, DCH * 256], BF16)

    rest = [p.sb(f"rest{i}", [128, TT], F32) for i in range(DCH)]
    x4 = [p.sb(f"x4_{i}", [128, TT], BF16) for i in range(DCH)]
    sq2 = [p.sb(f"sq{i}", [128, TT], BF16) for i in range(2)]
    rowset = {k: p.sb(f"r_{k}", [1, TT], F32) for k in ("srow", "qrow", "rstd", "negmur")}
    onec = p.sb("onec", [128, 1], BF16)
    oner = p.sb("oner", [1, 128], F32)
    bq = p.sb("bq", [128, 8], F32)
    bk = p.sb("bk", [128, 8], F32)
    bv = p.sb("bv", [128, 1024], F32)
    brq = p.sb("brq", [128, 2], F32)
    wblk = [p.sb(f"wblk{i}", [128, 1024], BF16) for i in range(2)]
    wvs = p.sb("wvs", [128, 1024], BF16)
    ob = p.sb("ob", [128, TT], BF16)
    obv = p.sb("obv", [128, 512], BF16)
    pcol_t = p.sb("pcol_t", [128, 16], F32)
    ps_s = p.psum("ps_s", [1, TT], F32)
    ps_q = p.psum("ps_q", [1, TT], F32)
    ps_r = p.psum("ps_r", [128, TT], F32)
    ps_m = p.psum("ps_m", [128, TT], F32)
    ps_mm = [p.psum(f"ps_mm{i}", [128, TT], F32) for i in range(2)]
    ps_v = p.psum("ps_v", [128, 512], F32)
    scr = {"sq": sq2, "ps_s": ps_s, "ps_q": ps_q, "onec": onec,
           "oner": oner, "ps_r": ps_r, "ps_m": ps_m}

    for t, s_ in [(bq, bq_in), (bk, bk_in), (bv, bv_in), (brq, brq_in)]:
        p.dio("sync", t[:], s_[:])
    for d_, s_ in [(wq_bf, wq_in), (wk_bf, wk_in), (wv_bf, wv_in), (wrq_bf, wrq_in)]:
        p.dio("gpsimd", d_[:], s_[:])
    p.op("vector", lambda e: e.memset(onec[:], 1.0), w=[onec])
    p.op("vector", lambda e: e.memset(oner[:], 1.0), w=[oner])

    for ti in range(NTT):
        c0 = ti * TT
        for i in range(DCH):
            p.dio("sync", rest[i][:], res_in[128 * i:128 * (i + 1), c0:c0 + TT], tag=f"res{ti}.{i}")
            p.op("vector", lambda e, i=i: e.tensor_copy(x4[i][:], rest[i][:]), r=[rest[i]], w=[x4[i]])
        e_ln(p, f"ln{ti}", x4, TT, rowset, scr)
        e_bc2(p, rowset, scr, TT)
        for i in range(DCH):
            p.op("vector", lambda e, i=i: e.tensor_mul(sq2[0][:, 0:TT], rest[i][:], ps_r[:, 0:TT]),
                 r=[rest[i], ps_r], w=[sq2[0]])
            p.op("vector", lambda e, i=i: e.tensor_add(x4[i][:], sq2[0][:, 0:TT], ps_m[:, 0:TT]),
                 r=[sq2[0], ps_m], w=[x4[i]])
        for i in range(DCH):
            for cch in range(4):
                li = ti * 4 + cch
                p.op("vector", lambda e, i=i, cch=cch, li=li:
                     e.reduce_sum(pcol_t[:, li:li + 1], x4[i][:, cch * 128:(cch + 1) * 128], axis=AX.X),
                     r=[x4[i]], w=[pcol_t], tag=f"pool{ti}.{i}.{cch}")
                p.op("vector", lambda e, i=i, li=li:
                     e.tensor_scalar_mul(pcol_t[:, li:li + 1], pcol_t[:, li:li + 1], 1.0 / 128),
                     r=[pcol_t], w=[pcol_t])
            p.dio("sync", pool_out[128 * i:128 * (i + 1), ti * 4:ti * 4 + 4], pcol_t[:, ti * 4:ti * 4 + 4],
                  tag=f"po{i}")
        for which, wbf, bias, outt, mch in (("q", wq_bf, bq, q_out, 8), ("k", wk_bf, bk, k_out, 8),
                                            ("rq", wrq_bf, brq, rq_out, 2)):
            for m in range(mch):
                wb = wblk[m % 2]
                p.dio("sync", wb[:], wbf[:, m * 1024:(m + 1) * 1024], tag=f"w{which}{m}")
                bank = ps_mm[m % 2]
                for k in range(DCH):
                    p.op("tensor", lambda e, k=k, wb=wb, bank=bank:
                         e.matmul(bank[:, 0:TT], wb[:, k * 128:(k + 1) * 128], x4[k][:],
                                  start=(k == 0), stop=(k == DCH - 1)),
                         r=[wb, x4[k]], w=[bank])
                p.op("scalar", lambda e, bank=bank, bias=bias, m=m:
                     e.add(ob[:], bank[:, 0:TT], bias[:, m:m + 1]),
                     r=[bank], sr=[bias], w=[ob])
                p.dio("sync", outt[128 * m:128 * (m + 1), c0:c0 + TT], ob[:], tag=f"o{which}{m}")
        for fg in range(2):
            for tch in range(4):
                tsl = slice(tch * 128, (tch + 1) * 128)
                for k in range(DCH):
                    wv2 = wblk[k % 2]
                    p.dio("sync", wv2[:, 0:512], wv_bf[:, k * 1024 + fg * 512:k * 1024 + fg * 512 + 512],
                          tag=f"wv{fg}.{tch}.{k}")
                    p.op("tensor", lambda e, k=k, tsl=tsl, wv2=wv2:
                         e.matmul(ps_v[:, 0:512], x4[k][:, tsl], wv2[:, 0:512],
                                  start=(k == 0), stop=(k == DCH - 1)),
                         r=[x4[k], wv2], w=[ps_v], tag=f"vmm{fg}.{k}.{tch}")
                p.op("vector", lambda e, fg=fg: e.tensor_add(obv[:, 0:512], ps_v[:, 0:512], bv[:, fg * 512:(fg + 1) * 512]),
                     r=[ps_v, bv], w=[obv])
                p.dio("sync", v_out[c0 + tch * 128:c0 + (tch + 1) * 128, fg * 512:(fg + 1) * 512],
                      obv[:, 0:512], tag=f"vo{fg}.{tch}")
    p.finish()
    return p



def host_l2_pre(res4_full, inputs):
    qkv = inputs["L2_attn_qkv_w"]
    qb = inputs["L2_attn_qkv_b"]
    def kmaj(w):
        return np.ascontiguousarray(w.reshape(DCH, 128, w.shape[1]).transpose(1, 0, 2).reshape(128, -1))
    com = {
        "wq_in": _pack_lhsT(qkv[:, 0:1024], 8, 8),
        "wk_in": _pack_lhsT(qkv[:, 1024:2048], 8, 8),
        "wv_in": kmaj(qkv[:, 2048:3072]),
        "wrq_in": _pack_lhsT(inputs["L2_attn_rq_w"], 8, 2),
        "bq_in": pcol(qb[0:1024]),
        "bk_in": pcol(qb[1024:2048]),
        "bv_in": np.ascontiguousarray(np.tile(qb[2048:3072][None, :], (128, 1))),
        "brq_in": pcol(inputs["L2_attn_rq_b"]),
    }
    in_maps = []
    for c in range(NCORES):
        b, s = cm(c)
        sl = slice(s * SPAN, (s + 1) * SPAN)
        m = dict(com)
        m["res_in"] = np.ascontiguousarray(res4_full[b, sl].T)
        in_maps.append(m)
    return in_maps


def build_l2_attn(strict=None, T=8192):
    p = Prog(strict)
    NQT = T // 128

    q_in = p.dram("q_in", [256, T], BF16, kind="ExternalInput")
    k_in = p.dram("k_in", [256, T], BF16, kind="ExternalInput")
    v_in = p.dram("v_in", [T, 256], BF16, kind="ExternalInput")
    rq_in = p.dram("rq_in", [64, T], BF16, kind="ExternalInput")
    pool_in = p.dram("pool_in", [D, 64], F32, kind="ExternalInput")
    wrk_in = p.dram("wrk_in", [128, DCH * 64], F32, kind="ExternalInput")
    wrv_in = p.dram("wrv_in", [128, DCH * 64], F32, kind="ExternalInput")
    rkb_in = p.dram("rkb_in", [64, 1], F32, kind="ExternalInput")
    rvb_in = p.dram("rvb_in", [64, 64], F32, kind="ExternalInput")
    mln_in = p.dram("mln_in", [128, 256], F32, kind="ExternalInput")
    ml0_in = p.dram("ml0_in", [128, 256], F32, kind="ExternalInput")
    wln_in = p.dram("wln_in", [128, 256], F32, kind="ExternalInput")
    wl0_in = p.dram("wl0_in", [128, 256], F32, kind="ExternalInput")
    lma_in = p.dram("lma_in", [NQT, 64], F32, kind="ExternalInput")
    lmm_in = p.dram("lmm_in", [NQT, 64], F32, kind="ExternalInput")
    ident_in = p.dram("ident_in", [128, 128], F32, kind="ExternalInput")
    loc_out = p.dram("loc_out", [T, 256], BF16, kind="ExternalOutput")
    lat_out = p.dram("lat_out", [64, T], BF16, kind="ExternalOutput")

    qsb = p.sb("qsb", [64, T], BF16)
    ksb = p.sb("ksb", [64, T], BF16)
    vb2 = [p.sb(f"vb{i}", [128, 64], BF16) for i in range(2)]
    mln = p.sb("mln", [128, 256], F32)
    ml0 = p.sb("ml0", [128, 256], F32)
    wln = p.sb("wln", [128, 256], F32)
    wl0 = p.sb("wl0", [128, 256], F32)
    ident = p.sb("ident", [128, 128], BF16)
    sct = p.sb("sct", [128, 256], F32)
    mxc = p.sb("mxc", [128, 1], F32)
    smc = p.sb("smc", [128, 1], F32)
    attb = p.sb("attb", [128, 256], BF16)
    attT = [p.sb(f"attT{i}", [128, 128], BF16) for i in range(2)]
    olc = p.sb("olc", [128, 64], BF16)
    rqsb = p.sb("rqsb", [64, T], BF16)
    poolb = [p.sb(f"poolb{i}", [128, 64], BF16) for i in range(DCH)]
    wrk = p.sb("wrk", [128, DCH * 64], BF16)
    wrv = p.sb("wrv", [128, DCH * 64], BF16)
    rkb = p.sb("rkb", [64, 1], F32)
    rvb = p.sb("rvb", [64, 64], F32)
    rks = p.sb("rks", [64, 64], BF16)
    rvs = p.sb("rvs", [64, 64], BF16)
    lmab = p.sb("lmab", [128, 64], F32)
    lmmb = p.sb("lmmb", [128, 64], F32)
    lsc = p.sb("lsc", [128, 64], F32)
    lab = p.sb("lab", [128, 64], BF16)
    laT = p.sb("laT", [64, 128], BF16)
    olat = p.sb("olat", [64, 128], BF16)

    ps_sc = p.psum("ps_sc", [128, 256], F32)
    ps_tr = p.psum("ps_tr", [128, 128], BF16)
    ps_o = p.psum("ps_o", [128, 64], F32)
    ps_rk = p.psum("ps_rk", [64, 64], F32)
    ps_rv = p.psum("ps_rv", [64, 64], F32)
    ps_ls = p.psum("ps_ls", [128, 64], F32)
    ps_lo = p.psum("ps_lo", [64, 128], F32)

    for t, s_ in [(mln, mln_in), (ml0, ml0_in), (wln, wln_in), (wl0, wl0_in),
                  (rkb, rkb_in), (rvb, rvb_in)]:
        p.dio("sync", t[:], s_[:])
    p.dio("gpsimd", ident[:], ident_in[:])
    p.dio("gpsimd", wrk[:], wrk_in[:])
    p.dio("gpsimd", wrv[:], wrv_in[:])
    p.dio("gpsimd", rqsb[:], rq_in[:])
    for i in range(DCH):
        p.dio("gpsimd", poolb[i][:], pool_in[128 * i:128 * (i + 1), :], tag=f"pb{i}")

    for k in range(DCH):
        p.op("tensor", lambda e, k=k: e.matmul(ps_rk[:], wrk[:, k * 64:(k + 1) * 64], poolb[k][:],
                                               start=(k == 0), stop=(k == DCH - 1)),
             r=[wrk, poolb[k]], w=[ps_rk])
    for k in range(DCH):
        p.op("tensor", lambda e, k=k: e.matmul(ps_rv[:], poolb[k][:], wrv[:, k * 64:(k + 1) * 64],
                                               start=(k == 0), stop=(k == DCH - 1)),
             r=[wrv, poolb[k]], w=[ps_rv])
    p.op("scalar", lambda e: e.add(rks[:], ps_rk[:], rkb[0:64, 0:1]), r=[ps_rk], sr=[rkb], w=[rks])
    p.op("vector", lambda e: e.tensor_add(lsc[0:64, 0:64], ps_rv[:], rvb[:]), r=[ps_rv, rvb], w=[lsc])
    p.op("vector", lambda e: e.tensor_copy(rvs[:], lsc[0:64, 0:64]), r=[lsc], w=[rvs])

    for u in range(4):
        r0 = u * 64
        p.dio("sync", qsb[:], q_in[r0:r0 + 64, :], tag="qsb")
        p.dio("sync", ksb[:], k_in[r0:r0 + 64, :], tag="ksb")
        for qt in range(NQT):
            t0 = qt * 128
            vcur = vb2[qt % 2]
            p.dio("sync", vcur[:], v_in[t0:t0 + 128, r0:r0 + 64], tag=f"v{qt % 2}")
            first = (qt == 0)
            madd = ml0 if first else mln
            mmul = wl0 if first else wln
            if first:
                p.op("tensor", lambda e, t0=t0: e.matmul(ps_sc[:, 128:256], qsb[:, t0:t0 + 128],
                                                         ksb[:, t0:t0 + 128], start=True, stop=True),
                     r=[qsb, ksb], w=[ps_sc])
            else:
                p.op("tensor", lambda e, t0=t0: e.matmul(ps_sc[:, 0:256], qsb[:, t0:t0 + 128],
                                                         ksb[:, t0 - 128:t0 + 128], start=True, stop=True),
                     r=[qsb, ksb], w=[ps_sc])
            p.op("vector", lambda e, madd=madd: e.scalar_tensor_tensor(sct[:], ps_sc[:], 0.125, madd[:],
                                                                       ALU.mult, ALU.add),
                 r=[ps_sc, madd], w=[sct])
            p.op("vector", lambda e: e.reduce_max(mxc[:], sct[:], axis=AX.X), r=[sct], w=[mxc])
            p.op("vector", lambda e: e.tensor_scalar_mul(mxc[:], mxc[:], -1.0), r=[mxc], w=[mxc])
            p.op("scalar", lambda e: e.activation(sct[:], sct[:], AF.Exp, bias=mxc[:, 0:1]),
                 r=[sct], sr=[mxc], w=[sct])
            p.op("vector", lambda e, mmul=mmul: e.tensor_mul(sct[:], sct[:], mmul[:]),
                 r=[sct, mmul], w=[sct])
            p.op("vector", lambda e: e.reduce_sum(smc[:], sct[:], axis=AX.X), r=[sct], w=[smc])
            p.op("vector", lambda e: e.tensor_scalar_add(smc[:], smc[:], 1e-30), r=[smc], w=[smc])
            p.op("vector", lambda e: e.reciprocal(smc[:], smc[:]), r=[smc], w=[smc])
            p.op("vector", lambda e: e.tensor_scalar_mul(attb[:], sct[:], smc[:, 0:1]),
                 r=[sct], sr=[smc], w=[attb])
            for half in range(2):
                p.op("tensor", lambda e, half=half: e.transpose(ps_tr[:], attb[:, half * 128:(half + 1) * 128],
                                                                ident[:]),
                     r=[attb, ident], w=[ps_tr])
                p.op("vector", lambda e, half=half: e.tensor_copy(attT[half][:], ps_tr[:]),
                     r=[ps_tr], w=[attT[half]])
            vprev = vb2[(qt + 1) % 2]
            if first:
                p.op("tensor", lambda e, vcur=vcur: e.matmul(ps_o[:], attT[1][:], vcur[:],
                                                             start=True, stop=True),
                     r=[attT[1], vcur], w=[ps_o])
            else:
                p.op("tensor", lambda e, vprev=vprev: e.matmul(ps_o[:], attT[0][:], vprev[:],
                                                               start=True, stop=False),
                     r=[attT[0], vprev], w=[ps_o])
                p.op("tensor", lambda e, vcur=vcur: e.matmul(ps_o[:], attT[1][:], vcur[:],
                                                             start=False, stop=True),
                     r=[attT[1], vcur], w=[ps_o])
            p.op("vector", lambda e: e.tensor_copy(olc[:], ps_o[:]), r=[ps_o], w=[olc])
            p.dio("sync", loc_out[t0:t0 + 128, r0:r0 + 64], olc[:], tag="olc")

    for qt in range(NQT):
        t0 = qt * 128
        p.dio("gpsimd", lmab[:], lma_in[qt:qt + 1, :].partition_broadcast(128), tag="lmab")
        p.dio("gpsimd", lmmb[:], lmm_in[qt:qt + 1, :].partition_broadcast(128), tag="lmmb")
        p.op("tensor", lambda e, t0=t0: e.matmul(ps_ls[:], rqsb[:, t0:t0 + 128], rks[:],
                                                 start=True, stop=True),
             r=[rqsb, rks], w=[ps_ls])
        p.op("vector", lambda e: e.scalar_tensor_tensor(lsc[:], ps_ls[:], 0.125, lmab[:],
                                                        ALU.mult, ALU.add),
             r=[ps_ls, lmab], w=[lsc])
        p.op("vector", lambda e: e.reduce_max(mxc[:], lsc[:], axis=AX.X), r=[lsc], w=[mxc])
        p.op("vector", lambda e: e.tensor_scalar_mul(mxc[:], mxc[:], -1.0), r=[mxc], w=[mxc])
        p.op("scalar", lambda e: e.activation(lsc[:], lsc[:], AF.Exp, bias=mxc[:, 0:1]),
             r=[lsc], sr=[mxc], w=[lsc])
        p.op("vector", lambda e: e.tensor_mul(lsc[:], lsc[:], lmmb[:]), r=[lsc, lmmb], w=[lsc])
        p.op("vector", lambda e: e.reduce_sum(smc[:], lsc[:], axis=AX.X), r=[lsc], w=[smc])
        p.op("vector", lambda e: e.tensor_scalar_add(smc[:], smc[:], 1e-30), r=[smc], w=[smc])
        p.op("vector", lambda e: e.reciprocal(smc[:], smc[:]), r=[smc], w=[smc])
        p.op("vector", lambda e: e.tensor_scalar_mul(lab[:], lsc[:], smc[:, 0:1]),
             r=[lsc], sr=[smc], w=[lab])
        p.op("tensor", lambda e: e.transpose(ps_tr[0:64, 0:128], lab[:], ident[:]),
             r=[lab, ident], w=[ps_tr])
        p.op("vector", lambda e: e.tensor_copy(laT[:], ps_tr[0:64, 0:128]), r=[ps_tr], w=[laT])
        p.op("tensor", lambda e: e.matmul(ps_lo[:], rvs[:], laT[:], start=True, stop=True),
             r=[rvs, laT], w=[ps_lo])
        p.op("vector", lambda e: e.tensor_copy(olat[:], ps_lo[:]), r=[ps_lo], w=[olat])
        p.dio("sync", lat_out[:, t0:t0 + 128], olat[:], tag="olat")
    p.finish()
    return p

def host_l2_attn(q_fm, k_fm, v_tm, rq_fm, pooled, inputs, T=8192):
    NQT = T // 128
    i_ = np.arange(128)[:, None]; j_ = np.arange(256)[None, :]
    allowN = (j_ >= i_) & (j_ <= i_ + 127)
    allow0 = allowN & (j_ >= 128)
    lt = np.arange(NQT)[:, None]; ll = np.arange(64)[None, :]
    lallow = ll < lt
    com = {
        "mln_in": np.where(allowN, 0, -1e9).astype(np.float32),
        "ml0_in": np.where(allow0, 0, -1e9).astype(np.float32),
        "wln_in": allowN.astype(np.float32),
        "wl0_in": allow0.astype(np.float32),
        "lma_in": np.where(lallow, 0, -1e9).astype(np.float32),
        "lmm_in": lallow.astype(np.float32),
        "ident_in": np.eye(128, dtype=np.float32),
    }
    rkv_w = inputs["L2_attn_rkv_w"]; rkv_b = inputs["L2_attn_rkv_b"]
    def kmaj(w):
        return np.ascontiguousarray(w.reshape(DCH, 128, w.shape[1]).transpose(1, 0, 2).reshape(128, -1))
    in_maps = []
    for c in range(NCORES):
        m = dict(com)
        units = [(4 * c + j) // 16 for j in range(4)], [(4 * c + j) % 16 for j in range(4)]
        bs, hs = units
        m["q_in"] = np.ascontiguousarray(np.concatenate(
            [q_fm[bs[j]][64 * hs[j]:64 * hs[j] + 64, :T] for j in range(4)], axis=0))
        m["k_in"] = np.ascontiguousarray(np.concatenate(
            [k_fm[bs[j]][64 * hs[j]:64 * hs[j] + 64, :T] for j in range(4)], axis=0))
        m["v_in"] = np.ascontiguousarray(np.concatenate(
            [v_tm[bs[j]][:T, 64 * hs[j]:64 * hs[j] + 64] for j in range(4)], axis=1))
        bl, lh = c // 4, c % 4
        m["rq_in"] = np.ascontiguousarray(rq_fm[bl][64 * lh:64 * lh + 64, :T])
        m["pool_in"] = pooled[bl]
        m["wrk_in"] = kmaj(rkv_w[:, 64 * lh:64 * lh + 64])
        m["wrv_in"] = kmaj(rkv_w[:, 256 + 64 * lh:256 + 64 * lh + 64])
        m["rkb_in"] = np.ascontiguousarray(rkv_b[64 * lh:64 * lh + 64].reshape(64, 1))
        m["rvb_in"] = np.ascontiguousarray(np.tile(rkv_b[256 + 64 * lh:256 + 64 * lh + 64][None, :], (64, 1)))
        in_maps.append(m)
    return in_maps




def build_l2_post(strict=None):
    p = Prog(strict)
    loc_in = p.dram("loc_in", [SPAN, D], BF16, kind="ExternalInput")
    lat_in = p.dram("lat_in", [256, SPAN], BF16, kind="ExternalInput")
    res_in = p.dram("res_in", [D, SPAN], F32, kind="ExternalInput")
    wro_in = p.dram("wro_in", [128, 2 * 1024], F32, kind="ExternalInput")
    bro_in = p.dram("bro_in", [128, 1024], F32, kind="ExternalInput")
    wout_in = p.dram("wout_in", [128, DCH * 1024], F32, kind="ExternalInput")
    bout_in = p.dram("bout_in", [128, DCH], F32, kind="ExternalInput")
    gl_in = p.dram("gl_in", [128, 64], F32, kind="ExternalInput")
    gr_in = p.dram("gr_in", [128, 64], F32, kind="ExternalInput")
    r_out = p.dram("r_out", [D, SPAN], F32, kind="ExternalOutput")
    wout_bf = p.dram("wout_bf", [128, DCH * 1024], BF16)

    lat = [p.sb(f"lat{i}", [128, SPAN], BF16) for i in range(2)]
    wro = p.sb("wro", [128, 2 * 1024], BF16)
    bro = p.sb("bro", [128, 1024], F32)
    glt = p.sb("glt", [128, 64], F32)
    grt = p.sb("grt", [128, 64], F32)
    bout = p.sb("bout", [128, DCH], F32)
    loct = p.sb("loct", [128, 1024], BF16)
    remt = p.sb("remt", [128, 1024], BF16)
    mixt = p.sb("mixt", [128, 1024], BF16)
    t1 = p.sb("t1", [128, 64], F32)
    t2 = p.sb("t2", [128, 64], F32)
    gcol = p.sb("gcol", [128, 1], F32)
    g2col = p.sb("g2col", [128, 1], F32)
    mix_fm = [p.sb(f"mixfm{i}", [128, TT], BF16) for i in range(DCH)]
    rt_ = p.sb("rt_", [128, TT], F32)
    ot = p.sb("ot", [128, TT], F32)
    wblk = [p.sb(f"wblk{i}", [128, 1024], BF16) for i in range(2)]
    ident = p.sb("ident", [128, 128], BF16)
    ident_in = p.dram("ident_in", [128, 128], F32, kind="ExternalInput")

    ps_r = [p.psum(f"ps_r{i}", [128, 512], F32) for i in range(2)]
    ps_tr = p.psum("ps_tr", [128, 128], BF16)
    ps_mm = [p.psum(f"ps_mm{i}", [128, TT], F32) for i in range(2)]

    p.dio("gpsimd", wro[:], wro_in[:])
    p.dio("gpsimd", ident[:], ident_in[:])
    p.dio("gpsimd", wout_bf[:], wout_in[:])
    p.dio("sync", bro[:], bro_in[:])
    p.dio("sync", glt[:], gl_in[:])
    p.dio("sync", grt[:], gr_in[:])
    p.dio("sync", bout[:], bout_in[:])
    for i in range(2):
        p.dio("sync", lat[i][:], lat_in[128 * i:128 * (i + 1), :], tag=f"lat{i}")

    for ti in range(NTT):
        c0 = ti * TT
        for tch in range(4):
            t0 = c0 + tch * 128
            p.dio("sync", loct[:], loc_in[t0:t0 + 128, :], tag="loct")
            for fg in range(2):
                for k in range(2):
                    p.op("tensor", lambda e, k=k, fg=fg, t0=t0:
                         e.matmul(ps_r[fg][:], lat[k][:, t0:t0 + 128], wro[:, k * 1024 + fg * 512:k * 1024 + (fg + 1) * 512],
                                  start=(k == 0), stop=(k == 1)),
                         r=[lat[k], wro], w=[ps_r[fg]], tag=f"rem{fg}")
                p.op("vector", lambda e, fg=fg: e.tensor_add(rt_[:, 0:512], ps_r[fg][:], bro[:, fg * 512:(fg + 1) * 512]),
                     r=[ps_r[fg], bro], w=[rt_])
                p.op("vector", lambda e, fg=fg: e.tensor_copy(remt[:, fg * 512:(fg + 1) * 512], rt_[:, 0:512]),
                     r=[rt_], w=[remt])
            for h in range(16):
                hs = slice(64 * h, 64 * h + 64)
                p.op("vector", lambda e, hs=hs: e.tensor_mul(t1[:], loct[:, hs], glt[:]),
                     r=[loct, glt], w=[t1])
                p.op("vector", lambda e: e.reduce_sum(gcol[:], t1[:], axis=AX.X), r=[t1], w=[gcol])
                p.op("vector", lambda e, hs=hs: e.tensor_mul(t1[:], remt[:, hs], grt[:]),
                     r=[remt, grt], w=[t1])
                p.op("vector", lambda e: e.reduce_sum(g2col[:], t1[:], axis=AX.X), r=[t1], w=[g2col])
                p.op("vector", lambda e: e.tensor_add(gcol[:], gcol[:], g2col[:]), r=[gcol, g2col], w=[gcol])
                p.op("scalar", lambda e: e.activation(gcol[:], gcol[:], AF.Sigmoid), r=[gcol], w=[gcol])
                p.op("vector", lambda e, hs=hs: e.tensor_sub(t1[:], loct[:, hs], remt[:, hs]),
                     r=[loct, remt], w=[t1])
                p.op("vector", lambda e: e.tensor_scalar_mul(t2[:], t1[:], gcol[:, 0:1]),
                     r=[t1], sr=[gcol], w=[t2])
                p.op("vector", lambda e, hs=hs: e.tensor_add(mixt[:, hs], remt[:, hs], t2[:]),
                     r=[remt, t2], w=[mixt])
            for fc in range(DCH):
                p.op("tensor", lambda e, fc=fc: e.transpose(ps_tr[:], mixt[:, fc * 128:(fc + 1) * 128], ident[:]),
                     r=[mixt, ident], w=[ps_tr])
                p.op("vector", lambda e, fc=fc, tch=tch: e.tensor_copy(mix_fm[fc][:, tch * 128:(tch + 1) * 128],
                                                                       ps_tr[:]),
                     r=[ps_tr], w=[mix_fm[fc]])
        for m in range(DCH):
            wb = wblk[m % 2]
            p.dio("sync", wb[:], wout_bf[:, m * 1024:(m + 1) * 1024], tag=f"wo{m}")
            bank = ps_mm[m % 2]
            for k in range(DCH):
                p.op("tensor", lambda e, k=k, wb=wb, bank=bank:
                     e.matmul(bank[:], wb[:, k * 128:(k + 1) * 128], mix_fm[k][:],
                              start=(k == 0), stop=(k == DCH - 1)),
                     r=[wb, mix_fm[k]], w=[bank])
            p.dio("sync", rt_[:], res_in[128 * m:128 * (m + 1), c0:c0 + TT], tag=f"res{m}")
            p.op("vector", lambda e, m=m, bank=bank:
                 e.scalar_tensor_tensor(ot[:], bank[:], bout[:, m:m + 1], rt_[:], ALU.add, ALU.add),
                 r=[bank, rt_], sr=[bout], w=[ot])
            p.dio("sync", r_out[128 * m:128 * (m + 1), c0:c0 + TT], ot[:], tag=f"ro{m}")
    p.finish()
    return p

def host_l2_post(loc_full, lat_full, res4_full, inputs):
    def kmaj2(w):
        return np.ascontiguousarray(w.reshape(2, 128, w.shape[1]).transpose(1, 0, 2).reshape(128, -1))
    com = {
        "wro_in": kmaj2(inputs["L2_attn_rout_w"]),
        "bro_in": np.ascontiguousarray(np.tile(inputs["L2_attn_rout_b"][None, :], (128, 1))),
        "wout_in": _pack_lhsT(inputs["L2_attn_out_w"], 8, 8),
        "bout_in": pcol(inputs["L2_attn_out_b"]),
        "gl_in": np.ascontiguousarray(np.tile(inputs["L2_attn_gl_w"][:, 0][None, :], (128, 1))),
        "gr_in": np.ascontiguousarray(np.tile(inputs["L2_attn_gr_w"][:, 0][None, :], (128, 1))),
        "ident_in": np.eye(128, dtype=np.float32),
    }
    in_maps = []
    for c in range(NCORES):
        b, s = cm(c)
        sl = slice(s * SPAN, (s + 1) * SPAN)
        m = dict(com)
        m["loc_in"] = np.ascontiguousarray(loc_full[b][sl])
        m["lat_in"] = np.ascontiguousarray(lat_full[b][:, sl])
        m["res_in"] = np.ascontiguousarray(res4_full[b, sl].T)
        in_maps.append(m)
    return in_maps


def build_l2_ffn(strict=None):
    p = Prog(strict)
    NE = 8
    res_in = p.dram("res_in", [D, SPAN], F32, kind="ExternalInput")
    w_rw = p.dram("w_rw", [128, DCH * 8], F32, kind="ExternalInput")
    b_rb = p.dram("b_rb", [8, 1], F32, kind="ExternalInput")
    ident_in = p.dram("ident_in", [128, 128], F32, kind="ExternalInput")
    w1_in = p.dram("w1_in", [128, NE * 32 * 1024], F32, kind="ExternalInput")
    w2_in = p.dram("w2_in", [128, NE * 8 * 4096], F32, kind="ExternalInput")
    b1_in = p.dram("b1_in", [128, NE * 32], F32, kind="ExternalInput")
    b2_in = p.dram("b2_in", [128, NE * 8], F32, kind="ExternalInput")
    r_out = p.dram("r_out", [D, SPAN], F32, kind="ExternalOutput")
    w1_bf = p.dram("w1_bf", [128, NE * 32 * 1024], BF16)
    w2_bf = p.dram("w2_bf", [128, NE * 8 * 4096], BF16)
    gates_hbm = p.dram("gates_hbm", [8, SPAN], F32)

    rest = [p.sb(f"rest{i}", [128, TT], F32) for i in range(DCH)]
    x2 = [p.sb(f"x2_{i}", [128, SPAN], BF16) for i in range(DCH)]
    hb = [p.sb(f"hb{i}", [128, TT], BF16) for i in range(32)]
    racc = [p.sb(f"racc{i}", [128, SPAN], F32) for i in range(DCH)]
    sq2 = [p.sb(f"sq{i}", [128, TT], BF16) for i in range(2)]
    rowset = {k: p.sb(f"r_{k}", [1, TT], F32) for k in ("srow", "qrow", "rstd", "negmur")}
    onec = p.sb("onec", [128, 1], BF16)
    oner = p.sb("oner", [1, 128], F32)
    rwt = p.sb("rwt", [128, DCH * 8], BF16)
    rbt = p.sb("rbt", [8, 1], F32)
    ident = p.sb("ident", [128, 128], BF16)
    b1t = p.sb("b1t", [128, NE * 32], F32)
    b2t = p.sb("b2t", [128, NE * 8], F32)
    lgt = p.sb("lgt", [8, TT], BF16)
    prt = p.sb("prt", [128, 8], F32)
    m1c = p.sb("m1c", [128, 1], F32)
    m2c = p.sb("m2c", [128, 1], F32)
    eq1 = p.sb("eq1", [128, 8], F32)
    gtm = p.sb("gtm", [128, 8], BF16)
    gfm = p.sb("gfm", [8, SPAN], F32)
    gbc = p.sb("gbc", [128, TT], F32)
    wbl = [p.sb(f"wbl{i}", [128, 1024], BF16) for i in range(2)]
    wbl3 = [p.sb(f"wbl3{i}", [128, 4096], BF16) for i in range(2)]
    ot = p.sb("ot", [128, TT], F32)

    ps_s = p.psum("ps_s", [1, TT], F32)
    ps_q = p.psum("ps_q", [1, TT], F32)
    ps_r = p.psum("ps_r", [128, TT], F32)
    ps_m = p.psum("ps_m", [128, TT], F32)
    ps_g = p.psum("ps_g", [128, TT], F32)
    ps_l = p.psum("ps_l", [8, TT], F32)
    ps_t = p.psum("ps_t", [128, 128], BF16)

    scr = {"sq": sq2, "ps_s": ps_s, "ps_q": ps_q, "onec": onec,
           "oner": oner, "ps_r": ps_r, "ps_m": ps_m}

    p.dio("gpsimd", rwt[:], w_rw[:])
    p.dio("sync", rbt[:], b_rb[:])
    p.dio("gpsimd", ident[:], ident_in[:])
    p.dio("sync", b1t[:], b1_in[:])
    p.dio("sync", b2t[:], b2_in[:])
    p.dio("gpsimd", w1_bf[:], w1_in[:])
    p.dio("gpsimd", w2_bf[:], w2_in[:])
    p.op("vector", lambda e: e.memset(onec[:], 1.0), w=[onec])
    p.op("vector", lambda e: e.memset(oner[:], 1.0), w=[oner])

    for ti in range(NTT):
        c0 = ti * TT
        for i in range(DCH):
            p.dio("sync", rest[i][:], res_in[128 * i:128 * (i + 1), c0:c0 + TT], tag=f"res{ti}.{i}")
            p.op("vector", lambda e, i=i, c0=c0: e.tensor_copy(x2[i][:, c0:c0 + TT], rest[i][:]),
                 r=[rest[i]], w=[x2[i]])
        e_ln(p, f"ln{ti}", [x2[i][:, c0:c0 + TT] for i in range(DCH)], TT, rowset, scr)
        e_bc2(p, rowset, scr, TT)
        for i in range(DCH):
            p.op("vector", lambda e, i=i: e.tensor_mul(sq2[0][:, 0:TT], rest[i][:], ps_r[:, 0:TT]),
                 r=[rest[i], ps_r], w=[sq2[0]])
            p.op("vector", lambda e, i=i, c0=c0: e.tensor_add(x2[i][:, c0:c0 + TT], sq2[0][:, 0:TT], ps_m[:, 0:TT]),
                 r=[sq2[0], ps_m], w=[x2[i]])
        for k in range(DCH):
            p.op("tensor", lambda e, k=k, c0=c0:
                 e.matmul(ps_l[:, 0:TT], rwt[:, k * 8:(k + 1) * 8], x2[k][:, c0:c0 + TT],
                          start=(k == 0), stop=(k == DCH - 1)),
                 r=[rwt, x2[k]], w=[ps_l])
        p.op("scalar", lambda e: e.add(lgt[:, 0:TT], ps_l[0:8, 0:TT], rbt[0:8, 0:1]),
             r=[ps_l], sr=[rbt], w=[lgt])
        for st in range(TT // 128):
            p.op("tensor", lambda e, st=st: e.transpose(ps_t[:, 0:8], lgt[0:8, st * 128:(st + 1) * 128],
                                                        ident[0:8, 0:8]),
                 r=[lgt, ident], w=[ps_t])
            p.op("vector", lambda e: e.tensor_copy(prt[:, 0:8], ps_t[:, 0:8]), r=[ps_t], w=[prt])
            p.op("vector", lambda e: e.reduce_max(m1c[:, 0:1], prt[:, 0:8], axis=AX.X), r=[prt], w=[m1c])
            p.op("vector", lambda e: e.tensor_scalar_mul(m1c[:, 0:1], m1c[:, 0:1], -1.0), r=[m1c], w=[m1c])
            p.op("scalar", lambda e: e.activation(prt[:, 0:8], prt[:, 0:8], AF.Exp, bias=m1c[:, 0:1]),
                 r=[prt], sr=[m1c], w=[prt])
            p.op("vector", lambda e: e.reduce_sum(m2c[:, 0:1], prt[:, 0:8], axis=AX.X), r=[prt], w=[m2c])
            p.op("vector", lambda e: e.reciprocal(m2c[:, 0:1], m2c[:, 0:1]), r=[m2c], w=[m2c])
            p.op("vector", lambda e: e.tensor_scalar_mul(prt[:, 0:8], prt[:, 0:8], m2c[:, 0:1]),
                 r=[prt], sr=[m2c], w=[prt])
            p.op("vector", lambda e: e.reduce_max(m1c[:, 0:1], prt[:, 0:8], axis=AX.X), r=[prt], w=[m1c])
            p.op("vector", lambda e: e.tensor_single_scalar(eq1[:, 0:8], prt[:, 0:8], m1c[:, 0:1], ALU.is_equal),
                 r=[prt], sr=[m1c], w=[eq1])
            p.op("vector", lambda e: e.tensor_scalar_mul(eq1[:, 0:8], eq1[:, 0:8], m1c[:, 0:1]),
                 r=[eq1], sr=[m1c], w=[eq1])
            p.op("vector", lambda e: e.tensor_copy(gtm[:, 0:8], eq1[:, 0:8]), r=[eq1], w=[gtm])
            p.op("tensor", lambda e: e.transpose(ps_t[0:8, 0:128], gtm[:, 0:8], ident[:]),
                 r=[gtm, ident], w=[ps_t])
            p.op("vector", lambda e, st=st, c0=c0: e.tensor_copy(gfm[0:8, c0 + st * 128:c0 + (st + 1) * 128],
                                                                 ps_t[0:8, 0:128]),
                 r=[ps_t], w=[gfm])
    p.dio("sync", gates_hbm[:], gfm[:])

    for u in range(NE):
        for ti in range(NTT):
            c0 = ti * TT
            for m in range(32):
                wg = wbl[m % 2]
                p.dio("sync", wg[:], w1_bf[:, (u * 32 + m) * 1024:(u * 32 + m + 1) * 1024], tag=f"wg{m % 2}")
                for k in range(DCH):
                    p.op("tensor", lambda e, k=k, wg=wg, c0=c0:
                         e.matmul(ps_g[:, 0:TT], wg[:, k * 128:(k + 1) * 128], x2[k][:, c0:c0 + TT],
                                  start=(k == 0), stop=(k == DCH - 1)),
                         r=[wg, x2[k]], w=[ps_g])
                p.op("vector", lambda e, u=u, m=m:
                     e.tensor_scalar(hb[m][:], ps_g[:, 0:TT], b1t[:, u * 32 + m:u * 32 + m + 1], 0.0,
                                     ALU.add, ALU.max),
                     r=[ps_g], sr=[b1t], w=[hb[m]])
            p.dio("gpsimd", gbc[:, 0:TT], gates_hbm[u:u + 1, c0:c0 + TT].partition_broadcast(128),
                  tag="gbc")
            for m in range(DCH):
                w3 = wbl3[m % 2]
                p.dio("sync", w3[:], w2_bf[:, (u * 8 + m) * 4096:(u * 8 + m + 1) * 4096], tag=f"w3{m % 2}")
                for k in range(32):
                    p.op("tensor", lambda e, k=k, w3=w3:
                         e.matmul(ps_g[:, 0:TT], w3[:, k * 128:(k + 1) * 128], hb[k][:],
                                  start=(k == 0), stop=(k == 31)),
                         r=[w3, hb[k]], w=[ps_g])
                p.op("vector", lambda e, u=u, m=m:
                     e.tensor_scalar_add(ot[:, 0:TT], ps_g[:, 0:TT], b2t[:, u * 8 + m:u * 8 + m + 1]),
                     r=[ps_g], sr=[b2t], w=[ot])
                p.op("vector", lambda e: e.tensor_mul(ot[:, 0:TT], ot[:, 0:TT], gbc[:, 0:TT]),
                     r=[ot, gbc], w=[ot])
                if u == 0:
                    p.op("vector", lambda e, m=m, c0=c0: e.tensor_copy(racc[m][:, c0:c0 + TT], ot[:, 0:TT]),
                         r=[ot], w=[racc[m]])
                else:
                    p.op("vector", lambda e, m=m, c0=c0: e.tensor_add(racc[m][:, c0:c0 + TT],
                                                                      racc[m][:, c0:c0 + TT], ot[:, 0:TT]),
                         r=[racc[m], ot], w=[racc[m]])
    for ti in range(NTT):
        c0 = ti * TT
        for m in range(DCH):
            p.dio("sync", rest[m][:], res_in[128 * m:128 * (m + 1), c0:c0 + TT], tag=f"res2.{ti}.{m}")
            p.op("vector", lambda e, m=m, c0=c0: e.tensor_add(rest[m][:], rest[m][:], racc[m][:, c0:c0 + TT]),
                 r=[rest[m], racc[m]], w=[rest[m]])
            p.dio("sync", r_out[128 * m:128 * (m + 1), c0:c0 + TT], rest[m][:], tag=f"ro{ti}.{m}")
    p.finish()
    return p

def host_l2_ffn(res5_full, inputs):
    NE = 8
    w1 = np.concatenate([_pack_lhsT(inputs["L2_ffn_W1"][e], 8, 32) for e in range(NE)], axis=1)
    w2 = np.concatenate([_pack_lhsT(inputs["L2_ffn_W2"][e], 32, 8) for e in range(NE)], axis=1)
    b1 = np.concatenate([pcol(inputs["L2_ffn_b1"][e]) for e in range(NE)], axis=1)
    b2 = np.concatenate([pcol(inputs["L2_ffn_b2"][e]) for e in range(NE)], axis=1)
    com = {
        "w1_in": w1, "w2_in": w2, "b1_in": b1, "b2_in": b2,
        "w_rw": np.ascontiguousarray(inputs["L2_ffn_Wr"].reshape(DCH, 128, 8).transpose(1, 0, 2).reshape(128, DCH * 8)),
        "b_rb": np.ascontiguousarray(inputs["L2_ffn_br"].reshape(8, 1)),
        "ident_in": np.eye(128, dtype=np.float32),
    }
    in_maps = []
    for c in range(NCORES):
        b, s = cm(c)
        sl = slice(s * SPAN, (s + 1) * SPAN)
        m = dict(com)
        m["res_in"] = np.ascontiguousarray(res5_full[b, sl].T)
        in_maps.append(m)
    return in_maps




def build_l3_pre(strict=None):
    p = Prog(strict)
    res_in = p.dram("res_in", [D, SPAN], F32, kind="ExternalInput")
    wb_in = p.dram("wb_in", [128, DCH * 64], F32, kind="ExternalInput")
    x7_out = p.dram("x7_out", [D, SPAN], BF16, kind="ExternalOutput")
    bu_out = p.dram("bu_out", [64, SPAN], F32, kind="ExternalOutput")

    rest = [p.sb(f"rest{i}", [128, TT], F32) for i in range(DCH)]
    x7 = [p.sb(f"x7_{i}", [128, TT], BF16) for i in range(DCH)]
    sq2 = [p.sb(f"sq{i}", [128, TT], BF16) for i in range(2)]
    rowset = {k: p.sb(f"r_{k}", [1, TT], F32) for k in ("srow", "qrow", "rstd", "negmur")}
    onec = p.sb("onec", [128, 1], BF16)
    oner = p.sb("oner", [1, 128], F32)
    wbt = p.sb("wbt", [128, DCH * 64], BF16)
    but = p.sb("but", [64, TT], F32)
    ps_s = p.psum("ps_s", [1, TT], F32)
    ps_q = p.psum("ps_q", [1, TT], F32)
    ps_r = p.psum("ps_r", [128, TT], F32)
    ps_m = p.psum("ps_m", [128, TT], F32)
    ps_b = p.psum("ps_b", [64, TT], F32)
    scr = {"sq": sq2, "ps_s": ps_s, "ps_q": ps_q, "onec": onec,
           "oner": oner, "ps_r": ps_r, "ps_m": ps_m}

    p.dio("gpsimd", wbt[:], wb_in[:])
    p.op("vector", lambda e: e.memset(onec[:], 1.0), w=[onec])
    p.op("vector", lambda e: e.memset(oner[:], 1.0), w=[oner])
    for ti in range(NTT):
        c0 = ti * TT
        for i in range(DCH):
            p.dio("sync", rest[i][:], res_in[128 * i:128 * (i + 1), c0:c0 + TT], tag=f"res{ti}.{i}")
            p.op("vector", lambda e, i=i: e.tensor_copy(x7[i][:], rest[i][:]), r=[rest[i]], w=[x7[i]])
        e_ln(p, f"ln{ti}", x7, TT, rowset, scr)
        e_bc2(p, rowset, scr, TT)
        for i in range(DCH):
            p.op("vector", lambda e, i=i: e.tensor_mul(sq2[0][:, 0:TT], rest[i][:], ps_r[:, 0:TT]),
                 r=[rest[i], ps_r], w=[sq2[0]])
            p.op("vector", lambda e, i=i: e.tensor_add(x7[i][:], sq2[0][:, 0:TT], ps_m[:, 0:TT]),
                 r=[sq2[0], ps_m], w=[x7[i]])
            p.dio("sync", x7_out[128 * i:128 * (i + 1), c0:c0 + TT], x7[i][:], tag=f"x7o{i}")
        for k in range(DCH):
            p.op("tensor", lambda e, k=k: e.matmul(ps_b[:, 0:TT], wbt[:, k * 64:(k + 1) * 64], x7[k][:],
                                                   start=(k == 0), stop=(k == DCH - 1)),
                 r=[wbt, x7[k]], w=[ps_b])
        p.op("vector", lambda e: e.tensor_copy(but[:], ps_b[0:64, 0:TT]), r=[ps_b], w=[but])
        p.dio("sync", bu_out[:, c0:c0 + TT], but[:], tag="buo")
    p.finish()
    return p

def host_l3_pre(res6_full, inputs):
    com = {"wb_in": _pack_lhsT(np.ascontiguousarray(inputs["L3_attn_B"].T), 8, 64 // 128 + 1)
           if False else None}
    wpack = np.ascontiguousarray(inputs["L3_attn_B"].T.reshape(DCH, 128, 64)
                                 .transpose(1, 0, 2).reshape(128, DCH * 64))
    com = {"wb_in": wpack}
    in_maps = []
    for c in range(NCORES):
        b, s = cm(c)
        sl = slice(s * SPAN, (s + 1) * SPAN)
        m = dict(com)
        m["res_in"] = np.ascontiguousarray(res6_full[b, sl].T)
        in_maps.append(m)
    return in_maps


def build_l3_scan(strict=None, T=8192):
    p = Prog(strict)
    TC = 2048
    bu_in = p.dram("bu_in", [64, T], F32, kind="ExternalInput")
    lam_in = p.dram("lam_in", [64, 1], F32, kind="ExternalInput")
    ldt_in = p.dram("ldt_in", [1, 1], F32, kind="ExternalInput")
    xs_out = p.dram("xs_out", [64, T], F32, kind="ExternalOutput")

    lam = p.sb("lam", [64, 1], F32)
    ldt = p.sb("ldt", [1, 1], F32)
    dtc = p.sb("dtc", [64, 1], F32)
    acol = p.sb("acol", [64, 1], F32)
    ones_t = p.sb("ones_t", [64, TC], F32)
    At = p.sb("At", [64, TC], F32)
    but = p.sb("but", [64, TC], F32)
    xst = p.sb("xst", [64, TC], F32)
    stc = p.sb("stc", [64, 1], F32)
    oner = p.sb("oner", [1, 128], F32)
    ps_g = p.psum("ps_g", [64, 1], F32)

    p.dio("sync", lam[:], lam_in[:])
    p.dio("sync", ldt[:], ldt_in[:])
    p.op("vector", lambda e: e.memset(oner[:], 1.0), w=[oner])
    p.op("vector", lambda e: e.memset(ones_t[:], 1.0), w=[ones_t])
    p.op("vector", lambda e: e.memset(stc[:], 0.0), w=[stc])
    p.op("scalar", lambda e: e.activation(ldt[:], ldt[:], AF.Exp), r=[ldt], w=[ldt])
    p.op("tensor", lambda e: e.matmul(ps_g[:], oner[0:1, 0:64], ldt[0:1, 0:1], start=True, stop=True),
         r=[oner, ldt], w=[ps_g])
    p.op("vector", lambda e: e.tensor_copy(dtc[:], ps_g[:]), r=[ps_g], w=[dtc])
    p.op("scalar", lambda e: e.activation(acol[:], lam[:], AF.Exp), r=[lam], w=[acol])
    p.op("vector", lambda e: e.tensor_mul(acol[:], acol[:], dtc[:]), r=[acol, dtc], w=[acol])
    p.op("scalar", lambda e: e.activation(acol[:], acol[:], AF.Exp, scale=-1.0), r=[acol], w=[acol])
    p.op("vector", lambda e: e.tensor_scalar_mul(At[:], ones_t[:], acol[0:64, 0:1]),
         r=[ones_t], sr=[acol], w=[At])
    for tc in range(T // TC):
        c0 = tc * TC
        p.dio("sync", but[:], bu_in[:, c0:c0 + TC], tag="but")
        p.op("vector", lambda e: e.tensor_tensor_scan(xst[:], At[:], but[:], stc[0:64, 0:1],
                                                      ALU.mult, ALU.add),
             r=[At, but], sr=[stc], w=[xst])
        p.op("vector", lambda e: e.tensor_copy(stc[:], xst[:, TC - 1:TC]), r=[xst], w=[stc])
        p.dio("sync", xs_out[:, c0:c0 + TC], xst[:], tag="xso")
    p.finish()
    return p

def host_l3_scan(bu_full, inputs, T=8192):
    in_maps = []
    for c in range(NCORES):
        b = c // NSPAN
        in_maps.append({
            "bu_in": np.ascontiguousarray(bu_full[b][:, :T]),
            "lam_in": np.ascontiguousarray(inputs["L3_attn_Lambda"].reshape(64, 1)),
            "ldt_in": np.ascontiguousarray(inputs["L3_attn_log_dt"].reshape(1, 1)),
        })
    return in_maps


def build_l3_post(strict=None):
    p = Prog(strict)
    xs_in = p.dram("xs_in", [64, SPAN], F32, kind="ExternalInput")
    x7_in = p.dram("x7_in", [D, SPAN], BF16, kind="ExternalInput")
    res_in = p.dram("res_in", [D, SPAN], F32, kind="ExternalInput")
    wc_in = p.dram("wc_in", [64, D], F32, kind="ExternalInput")
    dv_in = p.dram("dv_in", [128, DCH], F32, kind="ExternalInput")
    r_out = p.dram("r_out", [D, SPAN], F32, kind="ExternalOutput")

    xst = p.sb("xst", [64, TT], BF16)
    x7t = p.sb("x7t", [128, TT], BF16)
    rt = p.sb("rt", [128, TT], F32)
    ot = p.sb("ot", [128, TT], F32)
    wct = p.sb("wct", [64, D], BF16)
    dvt = p.sb("dvt", [128, DCH], F32)
    ps_mm = [p.psum(f"ps_mm{i}", [128, TT], F32) for i in range(2)]

    p.dio("gpsimd", wct[:], wc_in[:])
    p.dio("sync", dvt[:], dv_in[:])
    for ti in range(NTT):
        c0 = ti * TT
        p.dio("gpsimd", xst[:], xs_in[:, c0:c0 + TT], tag="xst")
        for m in range(DCH):
            bank = ps_mm[m % 2]
            p.op("tensor", lambda e, m=m, bank=bank: e.matmul(bank[:], wct[:, m * 128:(m + 1) * 128], xst[:],
                                                              start=True, stop=True),
                 r=[wct, xst], w=[bank])
            p.dio("sync", x7t[:], x7_in[128 * m:128 * (m + 1), c0:c0 + TT], tag="x7t")
            p.dio("sync", rt[:], res_in[128 * m:128 * (m + 1), c0:c0 + TT], tag="rt")
            p.op("vector", lambda e, m=m, bank=bank:
                 e.scalar_tensor_tensor(ot[:], x7t[:], dvt[:, m:m + 1], bank[:], ALU.mult, ALU.add),
                 r=[x7t, bank], sr=[dvt], w=[ot])
            p.op("vector", lambda e: e.tensor_add(ot[:], ot[:], rt[:]), r=[ot, rt], w=[ot])
            p.dio("sync", r_out[128 * m:128 * (m + 1), c0:c0 + TT], ot[:], tag=f"ro{m}")
    p.finish()
    return p

def host_l3_post(xs_full, x7_full, res6_full, inputs):
    com = {"wc_in": np.ascontiguousarray(inputs["L3_attn_C"].T),
           "dv_in": pcol(inputs["L3_attn_D"])}
    in_maps = []
    for c in range(NCORES):
        b, s = cm(c)
        sl = slice(s * SPAN, (s + 1) * SPAN)
        m = dict(com)
        m["xs_in"] = np.ascontiguousarray(xs_full[b][:, sl])
        m["x7_in"] = x7_full[c]
        m["res_in"] = np.ascontiguousarray(res6_full[b, sl].T)
        in_maps.append(m)
    return in_maps


def build_l3_ffn(strict=None):
    p = Prog(strict)
    res_in = p.dram("res_in", [D, SPAN], F32, kind="ExternalInput")
    wrT_in = p.dram("wrT_in", [128, DCH * 1024], F32, kind="ExternalInput")
    wiT_in = p.dram("wiT_in", [128, DCH * 1024], F32, kind="ExternalInput")
    gr_in = p.dram("gr_in", [128, DCH * 1024], F32, kind="ExternalInput")
    gin_in = p.dram("gin_in", [128, DCH * 1024], F32, kind="ExternalInput")
    gi_in = p.dram("gi_in", [128, DCH * 1024], F32, kind="ExternalInput")
    fr_in = p.dram("fr_in", [128, DCH * 1024], F32, kind="ExternalInput")
    fin_in = p.dram("fin_in", [128, DCH * 1024], F32, kind="ExternalInput")
    bias_in = p.dram("bias_in", [128, DCH], F32, kind="ExternalInput")
    r_out = p.dram("r_out", [D, SPAN], F32, kind="ExternalOutput")

    wrT_bf = p.dram("wrT_bf", [128, DCH * 1024], BF16)
    wiT_bf = p.dram("wiT_bf", [128, DCH * 1024], BF16)
    gr_bf = p.dram("gr_bf", [128, DCH * 1024], BF16)
    gin_bf = p.dram("gin_bf", [128, DCH * 1024], BF16)
    gi_bf = p.dram("gi_bf", [128, DCH * 1024], BF16)
    fr_bf = p.dram("fr_bf", [128, DCH * 1024], BF16)
    fin_bf = p.dram("fin_bf", [128, DCH * 1024], BF16)

    rest = [p.sb(f"rest{i}", [128, TT], F32) for i in range(DCH)]
    x8 = [p.sb(f"x8_{i}", [128, TT], BF16) for i in range(DCH)]
    sq2 = [p.sb(f"sq{i}", [128, TT], BF16) for i in range(2)]
    rowset = {k: p.sb(f"r_{k}", [1, TT], F32) for k in ("srow", "qrow", "rstd", "negmur")}
    onec = p.sb("onec", [128, 1], BF16)
    oner = p.sb("oner", [1, 128], F32)
    Ar = [p.sb(f"Ar{i}", [128, 1024], BF16) for i in range(DCH)]
    Ai = [p.sb(f"Ai{i}", [128, 1024], BF16) for i in range(DCH)]
    Msb = [p.sb(f"Msb{i}", [128, 1024], BF16) for i in range(DCH)]
    wblk = [p.sb(f"wblk{i}", [128, 1024], BF16) for i in range(4)]
    biast = p.sb("biast", [128, DCH], F32)
    ot = p.sb("ot", [128, TT], F32)
    ps_s = p.psum("ps_s", [1, TT], F32)
    ps_q = p.psum("ps_q", [1, TT], F32)
    ps_r = p.psum("ps_r", [128, TT], F32)
    ps_m = p.psum("ps_m", [128, TT], F32)
    ps_mm = [p.psum(f"ps_mm{i}", [128, TT], F32) for i in range(2)]
    scr = {"sq": sq2, "ps_s": ps_s, "ps_q": ps_q, "onec": onec,
           "oner": oner, "ps_r": ps_r, "ps_m": ps_m}

    p.dio("sync", biast[:], bias_in[:])
    for d_, s_ in [(wrT_bf, wrT_in), (wiT_bf, wiT_in), (gr_bf, gr_in), (gin_bf, gin_in),
                   (gi_bf, gi_in), (fr_bf, fr_in), (fin_bf, fin_in)]:
        p.dio("gpsimd", d_[:], s_[:])
    p.op("vector", lambda e: e.memset(onec[:], 1.0), w=[onec])
    p.op("vector", lambda e: e.memset(oner[:], 1.0), w=[oner])

    for kc in range(DCH):
        for half in range(2):
            msl = slice(half * 512, (half + 1) * 512)
            for oc in range(DCH):
                wr = wblk[0]; gr_ = wblk[1]; wi = wblk[2]; gi_ = wblk[3]
                p.dio("sync", wr[:, 0:128], wrT_bf[:, oc * 1024 + kc * 128:oc * 1024 + (kc + 1) * 128], tag="wr")
                p.dio("sync", gr_[:, 0:512], gr_bf[:, oc * 1024 + half * 512:oc * 1024 + half * 512 + 512], tag="gr_")
                p.dio("sync", wi[:, 0:128], wiT_bf[:, oc * 1024 + kc * 128:oc * 1024 + (kc + 1) * 128], tag="wi")
                p.dio("sync", gi_[:, 0:512], gin_bf[:, oc * 1024 + half * 512:oc * 1024 + half * 512 + 512], tag="gi_")
                p.op("tensor", lambda e, wr=wr, gr_=gr_, oc=oc:
                     e.matmul(ps_mm[0][:, 0:512], wr[:, 0:128], gr_[:, 0:512],
                              start=(oc == 0), stop=False),
                     r=[wr, gr_], w=[ps_mm[0]])
                p.op("tensor", lambda e, wi=wi, gi_=gi_, oc=oc:
                     e.matmul(ps_mm[0][:, 0:512], wi[:, 0:128], gi_[:, 0:512],
                              start=False, stop=(oc == DCH - 1)),
                     r=[wi, gi_], w=[ps_mm[0]])
            p.op("vector", lambda e, kc=kc, msl=msl: e.tensor_copy(Ar[kc][:, msl], ps_mm[0][:, 0:512]),
                 r=[ps_mm[0]], w=[Ar[kc]])
            for oc in range(DCH):
                wr = wblk[0]; gr_ = wblk[1]; wi = wblk[2]; gi_ = wblk[3]
                p.dio("sync", wr[:, 0:128], wrT_bf[:, oc * 1024 + kc * 128:oc * 1024 + (kc + 1) * 128], tag="wr")
                p.dio("sync", gi_[:, 0:512], gi_bf[:, oc * 1024 + half * 512:oc * 1024 + half * 512 + 512], tag="gi_")
                p.dio("sync", wi[:, 0:128], wiT_bf[:, oc * 1024 + kc * 128:oc * 1024 + (kc + 1) * 128], tag="wi")
                p.dio("sync", gr_[:, 0:512], gr_bf[:, oc * 1024 + half * 512:oc * 1024 + half * 512 + 512], tag="gr_")
                p.op("tensor", lambda e, wr=wr, gi_=gi_, oc=oc:
                     e.matmul(ps_mm[1][:, 0:512], wr[:, 0:128], gi_[:, 0:512],
                              start=(oc == 0), stop=False),
                     r=[wr, gi_], w=[ps_mm[1]])
                p.op("tensor", lambda e, wi=wi, gr_=gr_, oc=oc:
                     e.matmul(ps_mm[1][:, 0:512], wi[:, 0:128], gr_[:, 0:512],
                              start=False, stop=(oc == DCH - 1)),
                     r=[wi, gr_], w=[ps_mm[1]])
            p.op("vector", lambda e, kc=kc, msl=msl: e.tensor_copy(Ai[kc][:, msl], ps_mm[1][:, 0:512]),
                 r=[ps_mm[1]], w=[Ai[kc]])
    for dc in range(DCH):
        for half in range(2):
            msl = slice(half * 512, (half + 1) * 512)
            for kc in range(DCH):
                fr = wblk[0]; fi = wblk[1]
                p.dio("sync", fr[:, 0:128], fr_bf[:, kc * 1024 + dc * 128:kc * 1024 + (dc + 1) * 128], tag="fr")
                p.dio("sync", fi[:, 0:128], fin_bf[:, kc * 1024 + dc * 128:kc * 1024 + (dc + 1) * 128], tag="fi")
                p.op("tensor", lambda e, fr=fr, kc=kc, msl=msl:
                     e.matmul(ps_mm[0][:, 0:512], fr[:, 0:128], Ar[kc][:, msl],
                              start=(kc == 0), stop=False),
                     r=[fr, Ar[kc]], w=[ps_mm[0]])
                p.op("tensor", lambda e, fi=fi, kc=kc, msl=msl:
                     e.matmul(ps_mm[0][:, 0:512], fi[:, 0:128], Ai[kc][:, msl],
                              start=False, stop=(kc == DCH - 1)),
                     r=[fi, Ai[kc]], w=[ps_mm[0]])
            p.op("vector", lambda e, dc=dc, msl=msl: e.tensor_copy(Msb[dc][:, msl], ps_mm[0][:, 0:512]),
                 r=[ps_mm[0]], w=[Msb[dc]])
    for ti in range(NTT):
        c0 = ti * TT
        for i in range(DCH):
            p.dio("sync", rest[i][:], res_in[128 * i:128 * (i + 1), c0:c0 + TT], tag=f"res{ti}.{i}")
            p.op("vector", lambda e, i=i: e.tensor_copy(x8[i][:], rest[i][:]), r=[rest[i]], w=[x8[i]])
        e_ln(p, f"ln{ti}", x8, TT, rowset, scr)
        e_bc2(p, rowset, scr, TT)
        for i in range(DCH):
            p.op("vector", lambda e, i=i: e.tensor_mul(sq2[0][:, 0:TT], rest[i][:], ps_r[:, 0:TT]),
                 r=[rest[i], ps_r], w=[sq2[0]])
            p.op("vector", lambda e, i=i: e.tensor_add(x8[i][:], sq2[0][:, 0:TT], ps_m[:, 0:TT]),
                 r=[sq2[0], ps_m], w=[x8[i]])
        for m in range(DCH):
            bank = ps_mm[m % 2]
            for k in range(DCH):
                p.op("tensor", lambda e, k=k, m=m, bank=bank:
                     e.matmul(bank[:], Msb[k][:, m * 128:(m + 1) * 128], x8[k][:],
                              start=(k == 0), stop=(k == DCH - 1)),
                     r=[Msb[k], x8[k]], w=[bank])
            p.op("vector", lambda e, m=m, bank=bank:
                 e.scalar_tensor_tensor(ot[:], rest[m][:], 1.0, bank[:], ALU.mult, ALU.add),
                 r=[rest[m], bank], w=[ot])
            p.op("vector", lambda e, m=m: e.tensor_scalar_add(ot[:], ot[:], biast[:, m:m + 1]),
                 r=[ot], sr=[biast], w=[ot])
            p.dio("sync", r_out[128 * m:128 * (m + 1), c0:c0 + TT], ot[:], tag=f"ro{m}")
    p.finish()
    return p

def host_l3_ffn(res7_full, inputs):
    N = 1024
    dk = np.arange(N)
    ang = 2.0 * np.pi * np.outer(dk, dk % N) / N
    Fr = np.cos(-ang)
    Fi = np.sin(-ang)
    Gr = (np.cos(ang) / N)
    Gi = (np.sin(ang) / N)
    def kmaj(w):
        return np.ascontiguousarray(np.asarray(w, np.float32).reshape(DCH, 128, N)
                                    .transpose(1, 0, 2).reshape(128, DCH * N))
    com = {
        "wrT_in": kmaj(inputs["L3_ffn_w_freq_real"].T),
        "wiT_in": kmaj(inputs["L3_ffn_w_freq_imag"].T),
        "gr_in": kmaj(Gr), "gin_in": kmaj(-Gi), "gi_in": kmaj(Gi),
        "fr_in": kmaj(Fr), "fin_in": kmaj(-Fi),
        "bias_in": pcol(inputs["L3_ffn_bias"]),
    }
    in_maps = []
    for c in range(NCORES):
        b, s = cm(c)
        sl = slice(s * SPAN, (s + 1) * SPAN)
        m = dict(com)
        m["res_in"] = np.ascontiguousarray(res7_full[b, sl].T)
        in_maps.append(m)
    return in_maps


def _gather_fm(results, key, feat, dtype=np.float32):
    out = np.zeros((2, 8192, feat), dtype)
    for c in range(NCORES):
        b, s = cm(c)
        a = np.asarray(results[c][key])
        if a.dtype == np.uint16:
            a = a.view(NBF)
        out[b, s * SPAN:(s + 1) * SPAN] = a.T
    return out

def _run(name, build_fn, in_maps):
    nc = _cached(name, build_fn)
    return run_bass_kernel_spmd(nc, in_maps, core_ids=list(range(NCORES))).results

def kernel(**inputs):
    hid = np.ascontiguousarray(inputs["L0_attn_hidden"], dtype=np.float32)
    _HOST_CTX["A_log"] = inputs["L0_attn_A_log"]
    _HOST_CTX["D"] = inputs["L0_attn_D"]
    _HOST_CTX["out_w"] = inputs["L0_attn_out_w"]
    _HOST_CTX["out_b"] = inputs["L0_attn_out_b"]

    r = _run("l0_pre", build_l0_pre, host_l0_pre(inputs))
    z = _gather_fm(r, "z_out", D)
    xs = _gather_fm(r, "xs_out", 2048, NBF)
    dt = _gather_fm(r, "dt_out", 2048)
    bc = _gather_fm(r, "bc_out", 32)
    gate = _gather_fm(r, "gate_out", 2048, NBF)
    r = _run("l0_scan", build_l0_scan, host_l0_scan(dt, xs, bc))
    yfull = np.zeros((2, 8192, 2048), NBF)
    for c in range(NCORES):
        ch = slice(c * 256, (c + 1) * 256)
        for b in range(2):
            yfull[b, :, ch] = np.asarray(r[c][f"y{b}"]).view(NBF).T
    r = _run("l0_post", build_l0_post, host_l0_post(yfull, gate, z, hid))
    res = _gather_fm(r, "r_out", D)
    del z, xs, dt, bc, gate, yfull

    r = _run("l0_ffn", build_l0_ffn, host_l0_ffn(res, inputs))
    res = _gather_fm(r, "r_out", D)

    r = _run("l1_proj", build_l1_proj, host_l1_proj(res, inputs))
    q = _gather_fm(r, "q_out", 512, NBF)
    k_ = _gather_fm(r, "k_out", 512, NBF)
    v = _gather_fm(r, "v_out", 512, NBF)
    g = _gather_fm(r, "g_out", 512, NBF)
    r = _run("l1_attn", build_l1_attn, host_l1_attn(q, k_, v, g, inputs))
    yfull = np.zeros((2, 8192, 512), NBF)
    for c in range(NCORES):
        got = np.asarray(r[c]["y_out"]).view(NBF)
        for u in range(2):
            uid = c * 2 + u
            b, h = uid // 8, uid % 8
            yfull[b, :, 64 * h:64 * (h + 1)] = got[u * 64:(u + 1) * 64].T
    r = _run("l1_post", build_l1_post, host_l1_post(yfull, res, inputs))
    res = _gather_fm(r, "r_out", D)
    del q, k_, v, g, yfull

    r = _run("l1_ffn", build_l1_ffn, host_l1_ffn(res, inputs))
    res = _gather_fm(r, "r_out", D)

    r = _run("l2_pre", build_l2_pre, host_l2_pre(res, inputs))
    q_fm = {b: np.zeros((1024, 8192), NBF) for b in range(2)}
    k_fm = {b: np.zeros((1024, 8192), NBF) for b in range(2)}
    v_tm = {b: np.zeros((8192, 1024), NBF) for b in range(2)}
    rq_fm = {b: np.zeros((256, 8192), NBF) for b in range(2)}
    pooled = {b: np.zeros((1024, 64), np.float32) for b in range(2)}
    for c in range(NCORES):
        b, s = cm(c)
        sl = slice(s * SPAN, (s + 1) * SPAN)
        q_fm[b][:, sl] = np.asarray(r[c]["q_out"]).view(NBF)
        k_fm[b][:, sl] = np.asarray(r[c]["k_out"]).view(NBF)
        v_tm[b][sl, :] = np.asarray(r[c]["v_out"]).view(NBF)
        rq_fm[b][:, sl] = np.asarray(r[c]["rq_out"]).view(NBF)
        pooled[b][:, s * 16:(s + 1) * 16] = np.asarray(r[c]["pool_out"])
    r = _run("l2_attn", build_l2_attn, host_l2_attn(q_fm, k_fm, v_tm, rq_fm, pooled, inputs))
    loc_full = {b: np.zeros((8192, 1024), NBF) for b in range(2)}
    lat_full = {b: np.zeros((256, 8192), NBF) for b in range(2)}
    for c in range(NCORES):
        lo = np.asarray(r[c]["loc_out"]).view(NBF)
        for j in range(4):
            uid = 4 * c + j
            b, h = uid // 16, uid % 16
            loc_full[b][:, 64 * h:64 * h + 64] = lo[:, 64 * j:64 * j + 64]
        bl, lh = c // 4, c % 4
        lat_full[bl][64 * lh:64 * lh + 64, :] = np.asarray(r[c]["lat_out"]).view(NBF)
    r = _run("l2_post", build_l2_post, host_l2_post(loc_full, lat_full, res, inputs))
    res = _gather_fm(r, "r_out", D)
    del q_fm, k_fm, v_tm, rq_fm, pooled, loc_full, lat_full

    r = _run("l2_ffn", build_l2_ffn, host_l2_ffn(res, inputs))
    res = _gather_fm(r, "r_out", D)

    r = _run("l3_pre", build_l3_pre, host_l3_pre(res, inputs))
    bu_full = np.zeros((2, 64, 8192), np.float32)
    x7_cores = {}
    for c in range(NCORES):
        b, s = cm(c)
        bu_full[b][:, s * SPAN:(s + 1) * SPAN] = np.asarray(r[c]["bu_out"])
        x7_cores[c] = np.asarray(r[c]["x7_out"])
    r = _run("l3_scan", build_l3_scan, host_l3_scan(bu_full, inputs))
    xs_full = {0: np.asarray(r[0]["xs_out"]), 1: np.asarray(r[4]["xs_out"])}
    r = _run("l3_post", build_l3_post, host_l3_post(xs_full, x7_cores, res, inputs))
    res = _gather_fm(r, "r_out", D)

    r = _run("l3_ffn", build_l3_ffn, host_l3_ffn(res, inputs))
    res = _gather_fm(r, "r_out", D)
    return res.astype(np.float32)
